# version 19
# baseline (speedup 1.0000x reference)
import os
import numpy as np
import ml_dtypes
import concourse.bass as bass
import concourse.mybir as mybir
from concourse.bass_utils import run_bass_kernel_spmd

F32 = mybir.dt.float32
BF16 = mybir.dt.bfloat16
AF = mybir.ActivationFunctionType
ALU = mybir.AluOpType
AX = mybir.AxisListType

H = 1024
NH = 16
DH = 64
IF = 2816
CK = 4
T = 2048
GC = 256
NG = int(os.environ.get("BASS_NG", T // 128))
NCH = int(os.environ.get("BASS_NCH", 2))

ENGS = ("TE", "VE", "SE", "GP", "SY")


PAR = [0]
PAR_ACT = [0]
LBL = ["prep"]


class Trk:
    def __init__(self):
        self.sched = []
        self.cnt = dict.fromkeys(ENGS, 0)
        self.lw = {}
        self.rd = {}

    def op(self, eng, fn, r=(), w=(), xwaits=None):
        waits = dict(xwaits or {})

        def need(e2, c2):
            if e2 == eng:
                if eng in ("TE", "GP", "SY"):
                    return
                if self.cnt[eng] + 1 - c2 >= 12:
                    return
                if waits.get(e2, 0) < c2:
                    waits[e2] = c2
                return
            if e2 in ("GP", "SY"):
                c2 = self.cnt[e2]
            if waits.get(e2, 0) < c2:
                waits[e2] = c2

        for k in r:
            x = self.lw.get(k)
            if x:
                need(*x)
        for k in w:
            x = self.lw.get(k)
            if x:
                need(*x)
            for e2, c2 in self.rd.get(k, {}).items():
                need(e2, c2)
        self.cnt[eng] += 1
        c = self.cnt[eng]
        self.sched.append((eng, fn, waits, c, PAR[0]))
        self.lbl = getattr(self, 'lbl', []); self.lbl.append(LBL[0])
        for k in r:
            self.rd.setdefault(k, {})[eng] = c
        for k in w:
            self.lw[k] = (eng, c)
            self.rd[k] = {}
        return c

    def emit_plan(self):
        waited = {e: set() for e in ENGS}
        for eng, fn, waits, c, par in self.sched:
            for e2, c2 in waits.items():
                waited[e2].add(c2)
        semval = {}
        for e in ENGS:
            if e in ("GP", "SY"):
                continue
            acc, n = {}, 0
            for i in sorted(waited[e]):
                n += 1
                acc[i] = n
            semval[e] = acc

        def remap(e, c):
            return 16 * c if e in ("GP", "SY") else semval[e][c]

        per = {e: [] for e in ENGS}
        for eng, fn, waits, c, par in self.sched:
            inc = (eng in ("GP", "SY")) or (c in waited[eng])
            per[eng].append((fn, waits, inc, par))
        return per, remap, {e: waited[e] for e in ("GP", "SY")}


def build_program():
    nc = bass.Bass("TRN2", target_bir_lowering=False, detect_race_conditions=False)
    t = Trk()

    x_h = nc.dram_tensor("x", [2, T, H], F32, kind="ExternalInput").ap()
    w_h = {n: nc.dram_tensor(n, [H, H], F32, kind="ExternalInput").ap()
           for n in ("Wq", "Wk", "Wv", "Wo")}
    wb_h = nc.dram_tensor("Wb", [NH, H], F32, kind="ExternalInput").ap()
    cw_h = [nc.dram_tensor(n, [H, CK], F32, kind="ExternalInput").ap()
            for n in ("conv_q", "conv_k", "conv_v")]
    onw_h = nc.dram_tensor("o_norm_w", [DH], F32, kind="ExternalInput").ap()
    wg_h = nc.dram_tensor("w_gate", [IF, H], F32, kind="ExternalInput").ap()
    wu_h = nc.dram_tensor("w_up", [IF, H], F32, kind="ExternalInput").ap()
    wd_h = nc.dram_tensor("w_down", [H, IF], F32, kind="ExternalInput").ap()
    id128_h = nc.dram_tensor("id128", [128, 128], BF16, kind="ExternalInput").ap()
    su_h = nc.dram_tensor("su_stack", [128, 64], BF16, kind="ExternalInput").ap()
    sl_h = nc.dram_tensor("sl_stack", [128, 64], BF16, kind="ExternalInput").ap()
    ui_h = nc.dram_tensor("ui_stack", [128, 64], BF16, kind="ExternalInput").ap()
    ids_h = nc.dram_tensor("id_stack", [128, 64], BF16, kind="ExternalInput").ap()
    bdo_h = nc.dram_tensor("bdones", [128, 2], BF16, kind="ExternalInput").ap()
    out_h = nc.dram_tensor("out", [2, T, H], F32, kind="ExternalOutput").ap()
    DBG = os.environ.get("BASS_DBG") == "1"
    dbg_h = nc.dram_tensor("dbg", [6, 128, 2816], F32, kind="ExternalOutput").ap() if DBG else None
    wgT_h = nc.dram_tensor("wgT_s", [128, 8 * IF], BF16, kind="Internal").ap()
    wuT_h = nc.dram_tensor("wuT_s", [128, 8 * IF], BF16, kind="Internal").ap()
    wdT_h = nc.dram_tensor("wdT_s", [128, 22 * H], BF16, kind="Internal").ap()

    ctxs = []

    def sb(name, shape, dtype):
        c = nc.sbuf_tensor(name, shape, dtype)
        h = c.__enter__()
        ctxs.append(c)
        return h

    def psb(name, shape, dtype):
        c = nc.psum_tensor(name, shape, dtype)
        h = c.__enter__()
        ctxs.append(c)
        return h

    pbF = [psb(f"pbF{i}", [128, 512], F32) for i in range(6)]
    pbB = [psb(f"pbB{i}", [128, 1024], BF16) for i in range(2)]

    class Rot:

        def __init__(self, banks, bids, ncol, per):
            self.regs = []
            for i in range(per):
                for b, bid in zip(banks, bids):
                    self.regs.append((b, i * ncol, ("psb", bid)))
            self.i = 0

        def get(self, width):
            b, c0, k = self.regs[self.i % len(self.regs)]
            self.i += 1
            return b[:, c0:c0 + width], k

    rot_tr = Rot(pbB[0:2], (6, 7), 128, 1)
    rot_sol = Rot(pbF[2:6], (2, 3, 4, 5), 256, 1)
    rot_sm = Rot(pbF[0:4], (0, 1, 2, 3), 512, 1)
    rot_big = Rot(pbF[0:4], (0, 1, 2, 3), 512, 1)
    BK = [("psb", i) for i in range(8)]

    id128 = sb("id128_s", [128, 128], BF16)
    su = sb("su_s", [128, 64], BF16)
    slm = sb("slm", [128, 64], BF16)
    ui = sb("ui", [128, 64], BF16)
    ids = sb("ids", [128, 64], BF16)
    bdo = sb("bdo", [128, 2], BF16)
    onorm = sb("onorm", [128, 1], F32)
    zerop = sb("zerop", [128, 1], F32)
    epsln = sb("epsln", [128, 1], F32)
    epson = sb("epson", [128, 1], F32)

    wT = {n: sb(n + "T", [128, 64 * 128], BF16) for n in ("Wq", "Wk", "Wv")}
    woT = sb("woT", [128, 8 * 1024], BF16)
    wbT = sb("wbT", [128, 8 * 16], BF16)
    convw = sb("convw", [128, 3 * 32], F32)

    fstage = sb("fstage", [128, 1024], F32)

    q_abs = sb("q_abs", [128, 1024], F32)
    q_y = sb("q_y", [128, 1024], F32)
    q_a = sb("q_a", [128, 1024], BF16)
    q_b = sb("q_b", [128, 1024], BF16)
    q_sg = sb("q_sg", [128, 1024], BF16)
    q_gm = sb("q_gm", [128, 64], F32)
    q_sm = sb("q_sm", [128, 64], F32)
    q_in = sb("q_in", [128, 64], F32)


    xt = [sb(f"xt{i}", [128, 2 * H], F32) for i in range(2)]
    sums = sb("sums", [128, 4], F32)
    lnt = sb("lnt", [128, 12], F32)
    xn_tm = sb("xn_tm", [128, 2 * H], BF16)
    xnT = sb("xnT", [128, 8 * GC], BF16)
    qkv = sb("qkv", [128, 24 * GC], BF16)
    rawc = [sb(f"rawc{i}", [128, 262], BF16) for i in range(2)]
    halo = sb("halo", [128, 24 * 6], BF16)
    beta_p = sb("beta_p", [128, NCH * 16], F32)
    dkv = sb("dkv", [128, 2 * 8 * 16], F32)
    NKC, NU, DD, ND, NQ8, RR, OSC, SC1 = range(8)

    def dk(which, h0=0, n=16):
        o = PAR_ACT[0] * 128 + which * 16 + h0
        return dkv[:, o: o + n]

    maskD = sb("maskD", [128, 8 * 64], BF16)
    mqt = sb("mqt", [128, 8 * 64], BF16)
    zst = sb("zst", [128, 8 * 512], BF16)
    ktm = sb("ktm", [128, 8 * 64], BF16)
    ff = sb("ff", [128, 8 * 128], BF16)
    mst = sb("mst", [128, 8 * 64], BF16)
    qmp = sb("qmp", [128, 8 * 64], BF16)
    p1s = sb("p1s", [128, 8 * 64], BF16)
    css = sb("css", [128, 8 * 64], BF16)
    spair = sb("spair", [128, 16 * 64], BF16)
    o_pair = sb("o_pair", [128, 16 * 64], BF16)
    wnat = o_pair
    wbnat = spair[0:16, :]
    oT = sb("oT", [128, 8 * GC], BF16)
    x2 = sb("x2", [128, 2 * H], F32)
    xqT = sb("xqT", [128, 8 * GC], BF16)
    h_tm = sb("h_tm", [128, 2 * IF], BF16)
    hqT = sb("hqT", [128, 2 * 22 * 128], BF16)
    stg = hqT
    qtile = h_tm
    wgs = [sb("wgs0", [128, 8 * 352], BF16), sb("wgs1", [128, 8 * 352], BF16)]
    wus = [sb("wus0", [128, 8 * 352], BF16), sb("wus1", [128, 8 * 352], BF16)]
    wds = [sb("wds0", [128, 1024], BF16), sb("wds1", [128, 1024], BF16)]
    out_t = fstage

    def dma(q, dst, src, r=(), w=()):
        return t.op(q, lambda h, dst=dst, src=src: h.dma_start(dst, src), r=r, w=w)

    def mm(outp, lhsT, rhs, start, stop, tp=None, r=(), w=()):
        return t.op("TE", lambda h, o=outp, l=lhsT, rr=rhs, s1=start, s2=stop, tp=tp:
                    h.matmul(o, l, rr, start=s1, stop=s2, tile_position=tp), r=r, w=w)

    def tr(outp, inp, idt, r=(), w=()):
        return t.op("TE", lambda h, o=outp, i=inp, d=idt: h.transpose(o, i, d), r=r, w=w)

    def vo(fn, r=(), w=()):
        return t.op("VE", fn, r=r, w=w)

    def so(fn, r=(), w=()):
        return t.op("SE", fn, r=r, w=w)

    def go(fn, r=(), w=()):
        return t.op("GP", fn, r=r, w=w)

    KC = ("const",)

    dma("SY", id128[:], id128_h, w=[KC])
    dma("SY", su[:], su_h, w=[KC])
    dma("SY", slm[:], sl_h, w=[KC])
    dma("SY", ui[:], ui_h, w=[KC])
    dma("SY", ids[:], ids_h, w=[KC])
    dma("SY", bdo[:], bdo_h, w=[KC])
    dma("SY", onorm[0:64, :], onw_h.rearrange("(a b) -> a b", b=1), w=[KC])
    dma("SY", onorm[64:128, :], onw_h.rearrange("(a b) -> a b", b=1), w=[KC])
    vo(lambda h: h.memset(zerop[:], 0.0), w=[KC])
    vo(lambda h: h.memset(epsln[:], 1e-5), w=[KC])
    vo(lambda h: h.memset(epson[:], 1e-6), w=[KC])

    vo(lambda h: h.memset(halo[:], 0.0),
       w=[("halo", p, oc) for p in range(3) for oc in range(8)])

    for p in range(3):
        dma("GP", convw[:, p * 32:(p + 1) * 32].rearrange("p (c k) -> p c k", k=CK),
            cw_h[p].rearrange("(c p) k -> p c k", p=128), w=[KC])
    dma("GP", wbnat[:], wb_h, w=[("wbnat",)])

    def prep_attn(name, dstT, fold):
        for oc in range(8):
            dma("GP", wnat[:], w_h[name][oc * 128:(oc + 1) * 128, :], w=[("wnat",)])
            if True:
                for ic in range(8):
                    pt, pk = rot_tr.get(128)
                    tr(pt, wnat[:, ic * 128:(ic + 1) * 128],
                       id128[:], r=[("wnat",), KC], w=[pk])
                    if fold:
                        col = dstT[:, ic * 1024 + oc * 128: ic * 1024 + (oc + 1) * 128]
                        so(lambda h, pt=pt, col=col: h.activation(
                            col, pt, AF.Identity, bias=zerop[:], scale=onorm[:]),
                            r=[pk, KC], w=[(name, ic, oc)])
                    else:
                        col = dstT[:, (ic * 8 + oc) * 128: (ic * 8 + oc + 1) * 128]
                        so(lambda h, pt=pt, col=col: h.copy(col, pt),
                           r=[pk], w=[(name, ic, oc)])

    prep_attn("Wq", wT["Wq"], False)
    prep_attn("Wk", wT["Wk"], False)
    prep_attn("Wv", wT["Wv"], False)
    prep_attn("Wo", woT, True)

    for ic in range(8):
        pt, pk = rot_tr.get(16)
        tr(pt, wbnat[0:16, ic * 128:(ic + 1) * 128], id128[0:16, 0:16],
           r=[("wbnat",), KC], w=[pk])
        so(lambda h, pt=pt, ic=ic: h.copy(wbT[:, ic * 16:(ic + 1) * 16], pt),
           r=[pk], w=[("wbT", ic)])
    vo(lambda h: h.memset(spair[:], 0.0),
       w=[("sp", hh) for hh in range(16)] + [("wbnat",)])

    def quant(src, dst, width, rkeys, wkey):
        QS = ("qscr",)
        r = list(rkeys)
        nb = width // 16
        so(lambda h: h.activation(q_abs[:, :width], src, AF.Abs), r=r + [QS], w=[QS])
        so(lambda h: h.sign(q_sg[:, :width], src), r=r + [("qsg",)], w=[("qsg",)])
        vo(lambda h: h.tensor_reduce(
            q_gm[:, :nb].rearrange("p (g o) -> p g o", o=1),
            q_abs[:, :width].rearrange("p (g t) -> p g t", t=16), AX.X, ALU.max),
           r=[QS], w=[QS])
        vo(lambda h: h.tensor_scalar(q_sm[:, :nb], q_gm[:, :nb], 1.0 / 6.0, 1e-12, ALU.mult, ALU.max), r=[QS], w=[QS])
        vo(lambda h: h.reciprocal(q_in[:, :nb], q_sm[:, :nb]), r=[QS], w=[QS])
        invb = q_in[:, :nb].rearrange("p (g o) -> p g o", o=1).broadcast_to((128, nb, 16))
        smb = q_sm[:, :nb].rearrange("p (g o) -> p g o", o=1).broadcast_to((128, nb, 16))
        ab = q_abs[:, :width].rearrange("p (g t) -> p g t", t=16)
        yb = q_y[:, :width].rearrange("p (g t) -> p g t", t=16)
        vo(lambda h: h.tensor_tensor(yb, ab, invb, ALU.mult), r=[QS], w=[QS])
        y = q_y[:, :width]
        vo(lambda h: h.scalar_tensor_tensor(q_a[:, :width], y, 0.25, zerop[:, 0:1].broadcast_to((128, width)), ALU.is_gt, ALU.add), r=[QS, KC], w=[QS])
        for th in (0.75, 1.25, 1.75):
            vo(lambda h, th=th: h.scalar_tensor_tensor(q_a[:, :width], y, th, q_a[:, :width], ALU.is_gt, ALU.add), r=[QS], w=[QS])
        vo(lambda h: h.scalar_tensor_tensor(q_b[:, :width], y, 5.0, zerop[:, 0:1].broadcast_to((128, width)), ALU.is_gt, ALU.add), r=[QS, KC], w=[QS])
        vo(lambda h: h.scalar_tensor_tensor(q_b[:, :width], y, 5.0, q_b[:, :width], ALU.is_gt, ALU.add), r=[QS], w=[QS])
        for th in (2.5, 3.5):
            vo(lambda h, th=th: h.scalar_tensor_tensor(q_b[:, :width], y, th, q_b[:, :width], ALU.is_gt, ALU.add), r=[QS], w=[QS])
        vo(lambda h: h.scalar_tensor_tensor(q_a[:, :width], q_a[:, :width], 0.5, q_b[:, :width], ALU.mult, ALU.add), r=[QS], w=[QS])
        vo(lambda h: h.tensor_tensor(q_b[:, :width], q_a[:, :width], q_sg[:, :width], ALU.mult), r=[QS, ("qsg",)], w=[QS])
        qb2 = q_b[:, :width].rearrange("p (g t) -> p g t", t=16)
        db = dst.rearrange("p (g t) -> p g t", t=16)
        wk = wkey if isinstance(wkey, list) else [wkey]
        return vo(lambda h: h.tensor_tensor(db, qb2, smb, ALU.mult), r=[QS], w=wk)

    def prep_gu(hbm, dst_h, nm):
        for tt_ in range(22):
            dma("GP", fstage[:, :], hbm[tt_ * 128:(tt_ + 1) * 128, :],
                w=[("fst",)])
            quant(fstage[:, :], qtile[:, 0:1024], 1024, [("fst",)], ("qtile",))
            for ic in range(8):
                pt, pk = rot_tr.get(128)
                tr(pt, qtile[:, ic * 128:(ic + 1) * 128], id128[:],
                   r=[("qtile",), KC], w=[pk])
                so(lambda h, pt=pt, ic=ic: h.copy(stg[:, ic * 128:(ic + 1) * 128], pt),
                   r=[pk], w=[("stg", ic)])
            dma("SY", dst_h.rearrange("p (i c) -> p i c", c=IF)[:, :, tt_ * 128:(tt_ + 1) * 128],
                stg[:, 0:1024].rearrange("p (i c) -> p i c", c=128),
                r=[("stg", i2) for i2 in range(8)], w=[(nm, tt_)])

    t.mark = getattr(t, 'mark', {}); t.mark['prep_attn'] = len(t.sched)
    prep_gu(wg_h, wgT_h, "wgT_s")
    prep_gu(wu_h, wuT_h, "wuT_s")

    for oc in range(8):
        for piece, (c0, cw) in enumerate([(0, 1024), (1024, 1024), (2048, 768)]):
            dma("GP", fstage[:, 0:cw], wd_h[oc * 128:(oc + 1) * 128, c0:c0 + cw],
                w=[("fst",)])
            quant(fstage[:, 0:cw], qtile[:, c0:c0 + cw], cw,
                  [("fst",)], ("qtile", piece))
        for kc in range(22):
            pt, pk = rot_tr.get(128)
            tr(pt, qtile[:, kc * 128:(kc + 1) * 128], id128[:],
               r=[("qtile", 0), ("qtile", 1), ("qtile", 2), KC], w=[pk])
            so(lambda h, pt=pt, kc=kc: h.copy(stg[:, kc * 128:(kc + 1) * 128], pt),
               r=[pk], w=[("stg", kc)])
        dma("SY", wdT_h.rearrange("p (k c) -> p k c", c=H)[:, :, oc * 128:(oc + 1) * 128],
            stg[:, 0:IF].rearrange("p (k c) -> p k c", c=128),
            r=[("stg", k2) for k2 in range(22)], w=[("wdT_s", oc)])

    t.mark['prep'] = len(t.sched)
    WNM = ("Wq", "Wk", "Wv")

    def pair(buf, base, c):
        return buf[:, base:base + GC].rearrange("p (b r) -> p b r", b=2)[:, :, c * 64:(c + 1) * 64]

    def emit_ln(xs_key, xs, b, dst, dst_key):
        vo(lambda h: h.tensor_reduce(sums[:, b * 2:b * 2 + 1], xs, AX.X, ALU.add),
           r=[xs_key], w=[("sums", b)])
        so(lambda h: h.activation(q_abs[:, 0:H], xs, AF.Square),
           r=[xs_key, ("qscr",)], w=[("qscr",)])
        vo(lambda h: h.tensor_reduce(sums[:, b * 2 + 1:b * 2 + 2], q_abs[:, 0:H], AX.X, ALU.add),
           r=[("qscr",)], w=[("sums2", b)])
        L = lnt[:, b * 6:(b + 1) * 6]
        LK = ("lnt", b)
        vo(lambda h: h.tensor_scalar(L[:, 0:1], sums[:, b * 2:b * 2 + 1], 1.0 / H, None, ALU.mult),
           r=[("sums", b)], w=[LK])
        vo(lambda h: h.tensor_scalar(L[:, 1:2], sums[:, b * 2 + 1:b * 2 + 2], 1.0 / H, None, ALU.mult),
           r=[("sums2", b)], w=[LK])
        vo(lambda h: h.tensor_tensor(L[:, 2:3], L[:, 0:1], L[:, 0:1], ALU.mult), r=[LK], w=[LK])
        vo(lambda h: h.tensor_tensor(L[:, 2:3], L[:, 1:2], L[:, 2:3], ALU.subtract), r=[LK], w=[LK])
        so(lambda h: h.activation(L[:, 3:4], L[:, 2:3], AF.Sqrt, bias=epsln[:], scale=zerop[:].broadcast_to((128,1)) if False else 1.0), r=[LK, KC], w=[LK])
        vo(lambda h: h.reciprocal(L[:, 4:5], L[:, 3:4]), r=[LK], w=[LK])
        vo(lambda h: h.scalar_tensor_tensor(L[:, 5:6], L[:, 0:1], -1.0, L[:, 4:5], ALU.mult, ALU.mult),
           r=[LK], w=[LK])
        so(lambda h: h.activation(dst, xs, AF.Identity, bias=L[:, 5:6], scale=L[:, 4:5]),
           r=[LK, xs_key], w=[dst_key])

    for b in range(2):
        dma("GP", xt[0][:, b * H:(b + 1) * H], x_h[b, 0:128, :], w=[("xtl", 0, b)])
    for g in range(NG):
        par = g % 2
        XT = xt[par]
        XTK = [("xtl", par, 0), ("xtl", par, 1)]
        LBL[0] = 'xln'
        if g + 1 < NG:
            npar = (g + 1) % 2
            for b in range(2):
                dma("GP", xt[npar][:, b * H:(b + 1) * H],
                    x_h[b, (g + 1) * 128:(g + 2) * 128, :], w=[("xtl", npar, b)])
        for b in range(2):
            emit_ln(XTK[b], XT[:, b * H:(b + 1) * H], b,
                    xn_tm[:, b * H:(b + 1) * H], ("xn", b))
        for b in range(2):
            bank, bkk = pbB[b], ("psb", 6 + b)
            for ic in range(8):
                tr(bank[:, ic * 128:(ic + 1) * 128],
                   xn_tm[:, b * H + ic * 128: b * H + (ic + 1) * 128], id128[:],
                   r=[("xn", b), KC], w=[bkk])
            so(lambda h, bank=bank, b=b: h.copy(
                xnT[:, :].rearrange("p (i g2) -> p i g2", g2=GC)[:, :, b * 128:(b + 1) * 128],
                bank[:, 0:1024].rearrange("p (i t) -> p i t", t=128)),
                r=[bkk], w=[("xnT", ic2, b) for ic2 in range(8)])
        if DBG and g == int(os.environ.get('BASS_DBG_G', 0)):
            dma("GP", dbg_h[0, :, 0:2048], XT[:, 0:2048], r=[XTK[0], XTK[1]], w=[("dbgd", 0)])
            dma("GP", dbg_h[2, :, 0:12], lnt[:, 0:12], r=[("lnt", 0), ("lnt", 1)], w=[("dbgd", 2)])
            dma("GP", dbg_h[3, :, 0:2048], xn_tm[:, 0:2048], r=[("xn", 0), ("xn", 1)], w=[("dbgd", 3)])
        LBL[0] = 'beta'
        for c in range(NCH):
            pbm, pbk = rot_sm.get(16)
            for b in range(2):
                for ic in range(8):
                    mm(pbm[b * 64:(b + 1) * 64, 0:16],
                       xnT[:, ic * GC + b * 128 + c * 64: ic * GC + b * 128 + c * 64 + 64],
                       wbT[:, ic * 16:(ic + 1) * 16], ic == 0, ic == 7, tp=(0, b * 64),
                       r=[("xnT", ic, b), ("wbT", ic)], w=[pbk])
            so(lambda h, pbm=pbm, c=c: h.activation(
                beta_p[:, c * 16:(c + 1) * 16], pbm, AF.Sigmoid),
                r=[pbk], w=[("beta", c)])
        LBL[0] = 'qkvconv'
        for p in range(3):
            for oc in range(8):
                pg, pgk = rot_big.get(GC)
                for ic in range(8):
                    mm(pg, wT[WNM[p]][:, (ic * 8 + oc) * 128:(ic * 8 + oc + 1) * 128],
                       xnT[:, ic * GC:(ic + 1) * GC], ic == 0, ic == 7,
                       r=[("xnT", ic, 0), ("xnT", ic, 1), (WNM[p], ic, oc)], w=[pgk])
                rb = rawc[oc % 2]
                RK = ("rawc", oc % 2)
                HK = ("halo", p, oc)
                so(lambda h, pg=pg, rb=rb: h.copy(rb[:, 3:131], pg[:, 0:128]), r=[pgk], w=[RK])
                so(lambda h, pg=pg, rb=rb: h.copy(rb[:, 134:262], pg[:, 128:256]), r=[pgk], w=[RK])
                vo(lambda h, rb=rb, p=p, oc=oc: h.tensor_copy(
                    rb[:, 0:262].rearrange("p (s r) -> p s r", s=2)[:, :, 0:3],
                    halo[:, (p * 8 + oc) * 6:(p * 8 + oc) * 6 + 6].rearrange("p (s r) -> p s r", s=2)),
                   r=[HK], w=[RK])
                rb2 = rb[:, 0:262].rearrange("p (s r) -> p s r", s=2)
                acc2 = q_y[:, 0:256].rearrange("p (s r) -> p s r", s=2)
                wv_ = [convw[:, p * 32 + oc * 4 + k: p * 32 + oc * 4 + k + 1] for k in range(CK)]
                vo(lambda h, rb2=rb2, acc2=acc2, wv_=wv_: h.tensor_scalar(
                    acc2, rb2[:, :, 0:128], wv_[0][:], None, ALU.mult),
                   r=[RK, KC, ("qscr",)], w=[("qscr",)])
                for k2 in (1, 2, 3):
                    vo(lambda h, rb2=rb2, acc2=acc2, wv_=wv_, k2=k2: h.scalar_tensor_tensor(
                        acc2, rb2[:, :, k2:k2 + 128], wv_[k2][:], acc2, ALU.mult, ALU.add),
                       r=[RK, KC, ("qscr",)], w=[("qscr",)])
                vo(lambda h, rb=rb, p=p, oc=oc: h.tensor_copy(
                    halo[:, (p * 8 + oc) * 6:(p * 8 + oc) * 6 + 6].rearrange("p (s r) -> p s r", s=2),
                    rb[:, 0:262].rearrange("p (s r) -> p s r", s=2)[:, :, 128:131]),
                   r=[RK], w=[HK])
                so(lambda h, p=p, oc=oc: h.activation(
                    qkv[:, (p * 8 + oc) * GC:(p * 8 + oc + 1) * GC], q_y[:, 0:256], AF.Silu),
                   r=[("qscr",)], w=[("qkv", p, oc)])

        if g == 0: t.mark['g0_attnpre'] = len(t.sched)

        for c in range(NCH):
            PAR[0] = (g * NCH + c) % 2
            LBL[0] = 'norm'
            pnq, pnqk = rot_sm.get(16)
            pnk, pnkk = rot_sm.get(16)
            for hp in range(8):
                st_, sp_ = hp == 0, hp == 7
                so(lambda h, hp=hp, c=c: h.activation(
                    q_a[:, hp * 128:(hp + 1) * 128].rearrange("p (b r) -> p b r", b=2),
                    pair(qkv, (0 * 8 + hp) * GC, c), AF.Square),
                   r=[("qkv", 0, hp), ("qscr",)], w=[("sqq", hp), ("qscr",)])
                so(lambda h, hp=hp, c=c: h.activation(
                    q_b[:, hp * 128:(hp + 1) * 128].rearrange("p (b r) -> p b r", b=2),
                    pair(qkv, (1 * 8 + hp) * GC, c), AF.Square),
                   r=[("qkv", 1, hp), ("qscr",)], w=[("sqk", hp), ("qscr",)])
                mm(pnq[:, hp * 2:hp * 2 + 2], q_a[:, hp * 128:(hp + 1) * 128], bdo[:],
                   st_, sp_, r=[("sqq", hp), KC], w=[pnqk])
                mm(pnk[:, hp * 2:hp * 2 + 2], q_b[:, hp * 128:(hp + 1) * 128], bdo[:],
                   st_, sp_, r=[("sqk", hp), KC], w=[pnkk])
            DKK = ("dkv", (g * NCH + c) % 2)
            so(lambda h, pnk=pnk: h.activation(dk(SC1), pnk, AF.Sqrt), r=[pnkk, DKK], w=[DKK])
            vo(lambda h: h.tensor_scalar(dk(NKC), dk(SC1), 1e-6, None, ALU.max), r=[DKK], w=[DKK])
            vo(lambda h: h.reciprocal(dk(NU), dk(NKC)), r=[DKK], w=[DKK])
            vo(lambda h: h.tensor_tensor(dk(DD), dk(NU), dk(NU), ALU.mult), r=[DKK], w=[DKK])
            vo(lambda h, c=c: h.tensor_tensor(dk(DD), dk(DD), beta_p[:, c * 16:(c + 1) * 16], ALU.mult),
               r=[DKK, ("beta", c)], w=[DKK])
            vo(lambda h: h.tensor_scalar(dk(ND), dk(DD), -1.0, None, ALU.mult), r=[DKK], w=[DKK])
            so(lambda h, pnq=pnq: h.activation(dk(SC1), pnq, AF.Sqrt), r=[pnqk, DKK], w=[DKK])
            vo(lambda h: h.tensor_scalar(dk(SC1), dk(SC1), 1e-6, None, ALU.max), r=[DKK], w=[DKK])
            vo(lambda h: h.reciprocal(dk(SC1), dk(SC1)), r=[DKK], w=[DKK])
            vo(lambda h: h.tensor_scalar(dk(NQ8), dk(SC1), 0.125, None, ALU.mult), r=[DKK], w=[DKK])

            for blk in range(2):
                LBL[0] = 'kvtr'
                heads = list(range(blk * 8, blk * 8 + 8))
                for hp in range(blk * 4, blk * 4 + 4):
                    pv, pvk = rot_tr.get(128)
                    pk_, pkk = rot_tr.get(128)
                    for b in range(2):
                        col = (2 * 8 + hp) * GC + b * 128 + c * 64
                        t.op("TE", lambda h, pv=pv, col=col, b=b: h.transpose(
                            pv[b * 64:(b + 1) * 64, :], qkv[:, col:col + 64], id128[:],
                            tile_position=(0, b * 64)),
                            r=[("qkv", 2, hp), KC], w=[pvk])
                        col2 = (1 * 8 + hp) * GC + b * 128 + c * 64
                        t.op("TE", lambda h, pk_=pk_, col2=col2, b=b: h.transpose(
                            pk_[b * 64:(b + 1) * 64, :], qkv[:, col2:col2 + 64], id128[:],
                            tile_position=(0, b * 64)),
                            r=[("qkv", 1, hp), KC], w=[pkk])
                    for h_ in (2 * hp, 2 * hp + 1):
                        sl = h_ % 8
                        Z = zst[:, sl * 512: sl * 512 + 512]
                        ZK = ("zst", sl)
                        hcol = (h_ % 2) * 64
                        vo(lambda h, pv=pv, Z=Z, hcol=hcol, h_=h_: h.tensor_scalar(
                            Z[:, 0:64], pv[:, hcol:hcol + 64], dk(NKC, h_, 1), None, ALU.mult),
                           r=[pvk, DKK], w=[ZK])
                        so(lambda h, pk_=pk_, Z=Z, hcol=hcol: h.copy(Z[:, 64:128], pk_[:, hcol:hcol + 64]),
                           r=[pkk], w=[ZK])
                        so(lambda h, pk_=pk_, hcol=hcol, sl=sl: h.copy(
                            ktm[:, sl * 64:(sl + 1) * 64], pk_[:, hcol:hcol + 64]),
                           r=[pkk], w=[("ktm", sl)])
                LBL[0] = 'gmask'
                ginfo = {}
                for h_ in heads:
                    hp, hh = h_ // 2, 64 * (h_ % 2)
                    sl = h_ % 8
                    Z = zst[:, sl * 512: sl * 512 + 512]
                    ZK = ("zst", sl)
                    pg_, pgk2 = rot_sm.get(128)
                    for b in range(2):
                        col = hp * GC + b * 128 + c * 64
                        lhs = qkv[hh:hh + 64, 1 * 8 * GC + col: 1 * 8 * GC + col + 64]
                        qsrc = qkv[hh:hh + 64, col: col + 64]
                        ksrc = qkv[hh:hh + 64, 1 * 8 * GC + col: 1 * 8 * GC + col + 64]
                        mm(pg_[b * 64:(b + 1) * 64, 0:64], lhs, qsrc, True, False,
                           tp=(hh, b * 64),
                           r=[("qkv", 0, hp), ("qkv", 1, hp)], w=[pgk2])
                        mm(pg_[b * 64:(b + 1) * 64, 64:128], lhs, ksrc, False, True,
                           tp=(hh, b * 64),
                           r=[("qkv", 1, hp)], w=[pgk2])
                    vo(lambda h, sl=sl, h_=h_: h.tensor_scalar(
                        maskD[:, sl * 64:(sl + 1) * 64], su[:], dk(ND, h_, 1), None, ALU.mult),
                       r=[KC, DKK], w=[("maskD", sl)])
                    vo(lambda h, pg_=pg_, Z=Z, sl=sl: h.tensor_tensor(
                        Z[:, 192:256], pg_[:, 64:128], maskD[:, sl * 64:(sl + 1) * 64], ALU.mult),
                       r=[pgk2, ("maskD", sl)], w=[ZK])
                    vo(lambda h, pg_=pg_, Z=Z: h.tensor_tensor(
                        Z[:, 128:192], pg_[:, 64:128], slm[:], ALU.mult),
                       r=[pgk2, KC], w=[ZK])
                    vo(lambda h, pg_=pg_, sl=sl: h.tensor_tensor(
                        mqt[:, sl * 64:(sl + 1) * 64], pg_[:, 0:64], ui[:], ALU.mult),
                       r=[pgk2, KC], w=[("mqt", sl)])
                    ginfo[h_] = (Z, ZK, sl)
                LBL[0] = 'solve'
                for i in range(6):
                    for h_ in heads:
                        Z, ZK, sl = ginfo[h_]
                        src = Z[:, (i % 2) * 256:(i % 2) * 256 + 256]
                        pa, pak = rot_sol.get(256)
                        wdt = 192 if i < 5 else 128
                        for b in range(2):
                            mm(pa[b * 64:(b + 1) * 64, 0:wdt],
                               src[b * 64:(b + 1) * 64, 192:256],
                               src[b * 64:(b + 1) * 64, 0:wdt],
                               True, i == 5, tp=(b * 64, b * 64), r=[ZK], w=[pak])
                            if i < 5:
                                mm(pa[b * 64:(b + 1) * 64, 192:256],
                                   src[b * 64:(b + 1) * 64, 128:192],
                                   src[b * 64:(b + 1) * 64, 192:256],
                                   False, True, tp=(b * 64, b * 64), r=[ZK], w=[pak])
                        if i < 5:
                            dstz = Z[:, ((i + 1) % 2) * 256:((i + 1) % 2) * 256 + 256]
                            vo(lambda h, pa=pa, src=src, dstz=dstz: h.tensor_tensor(
                                dstz[:, 0:128], pa[:, 0:128], src[:, 0:128], ALU.add),
                               r=[pak, ZK], w=[ZK])
                            so(lambda h, pa=pa, dstz=dstz: h.copy(dstz[:, 128:192], pa[:, 128:192]),
                               r=[pak], w=[ZK])
                            so(lambda h, pa=pa, dstz=dstz, h_=h_: h.activation(
                                dstz[:, 192:256], pa[:, 192:256], AF.Identity,
                                bias=zerop[:], scale=dk(ND, h_, 1)),
                               r=[pak, DKK], w=[ZK])
                        else:
                            vo(lambda h, pa=pa, src=src, sl=sl: h.tensor_tensor(
                                ff[:, sl * 128:(sl + 1) * 128], pa[:, 0:128], src[:, 0:128], ALU.add),
                               r=[pak, ZK], w=[("ff", sl)])
                LBL[0] = 'pphase'
                for sub in range(2):
                    for h_ in heads[sub * 4:sub * 4 + 4]:
                        hp, hh = h_ // 2, 64 * (h_ % 2)
                        sl = h_ % 8
                        FK = ("ff", sl)
                        vo(lambda h, sl=sl, h_=h_: h.tensor_scalar(
                            ff[:, sl * 128:sl * 128 + 64], ff[:, sl * 128:sl * 128 + 64],
                            dk(DD, h_, 1), None, ALU.mult), r=[FK, DKK], w=[FK])
                        vo(lambda h, sl=sl, h_=h_: h.tensor_scalar(
                            ff[:, sl * 128 + 64:sl * 128 + 128], ff[:, sl * 128 + 64:sl * 128 + 128],
                            dk(DD, h_, 1), None, ALU.mult), r=[FK, DKK], w=[FK])
                        UV = ff[:, sl * 128:sl * 128 + 64]
                        WB = ff[:, sl * 128 + 64:sl * 128 + 128]
                        MQ = mqt[:, sl * 64:(sl + 1) * 64]
                        KT = ktm[:, sl * 64:(sl + 1) * 64]
                        SP = spair[:, h_ * 64:(h_ + 1) * 64]
                        psess, psk = rot_sm.get(256)
                        po, pok = psess[:, 0:64], psk
                        pc_, pck = psess[:, 64:128], psk
                        pp, ppk = psess[:, 128:192], psk
                        pm, pmk = psess[:, 192:256], psk
                        for b in range(2):
                            mm(po[b * 64:(b + 1) * 64, :], MQ[b * 64:(b + 1) * 64, :],
                               UV[b * 64:(b + 1) * 64, :], True, False, tp=(b * 64, b * 64),
                               r=[("mqt", sl), FK], w=[pok])
                            mm(pc_[b * 64:(b + 1) * 64, :], KT[b * 64:(b + 1) * 64, :],
                               UV[b * 64:(b + 1) * 64, :], False, False, tp=(b * 64, b * 64),
                               r=[("ktm", sl), FK], w=[pck])
                            mm(pp[b * 64:(b + 1) * 64, :], WB[b * 64:(b + 1) * 64, :],
                               MQ[b * 64:(b + 1) * 64, :], False, False, tp=(b * 64, b * 64),
                               r=[FK, ("mqt", sl)], w=[ppk])
                            mm(pm[b * 64:(b + 1) * 64, :], WB[b * 64:(b + 1) * 64, :],
                               KT[b * 64:(b + 1) * 64, :], False, True, tp=(b * 64, b * 64),
                               r=[FK, ("ktm", sl)], w=[pmk])
                        vo(lambda h, po=po, sl=sl, h_=h_: h.tensor_scalar(
                            p1s[:, sl * 64:(sl + 1) * 64], po[:, 0:64], dk(NQ8, h_, 1), None, ALU.mult),
                           r=[pok, DKK], w=[("p1s", sl)])
                        vo(lambda h, pc_=pc_, sl=sl: h.tensor_copy(css[:, sl * 64:(sl + 1) * 64], pc_[:, 0:64]),
                           r=[pck], w=[("css", sl)])
                        for b in range(2):
                            qsl = qkv[hh:hh + 64, hp * GC + b * 128 + c * 64:
                                      hp * GC + b * 128 + c * 64 + 64]
                            vo(lambda h, qsl=qsl, pp=pp, b=b, sl=sl: h.tensor_tensor(
                                qmp[b * 64:(b + 1) * 64, sl * 64:(sl + 1) * 64], qsl,
                                pp[b * 64:(b + 1) * 64, 0:64], ALU.subtract),
                               r=[ppk, ("qkv", 0, hp)], w=[("qmp", sl)])
                        vo(lambda h, pm=pm, sl=sl: h.scalar_tensor_tensor(
                            mst[:, sl * 64:(sl + 1) * 64], pm[:, 0:64], -1.0, ids[:], ALU.mult, ALU.add),
                           r=[pmk, KC], w=[("mst", sl)])
                        ssess, ssk = rot_sm.get(128)
                        po2, po2k = ssess[:, 0:64], ssk
                        ps2_, ps2k = ssess[:, 64:128], ssk
                        for b in range(2):
                            mm(po2[b * 64:(b + 1) * 64, :], qmp[b * 64:(b + 1) * 64, sl * 64:(sl + 1) * 64],
                               SP[b * 64:(b + 1) * 64, :], True, False, tp=(b * 64, b * 64),
                               r=[("qmp", sl), ("sp", h_)], w=[po2k])
                            mm(ps2_[b * 64:(b + 1) * 64, :], mst[b * 64:(b + 1) * 64, sl * 64:(sl + 1) * 64],
                               SP[b * 64:(b + 1) * 64, :], False, True, tp=(b * 64, b * 64),
                               r=[("mst", sl), ("sp", h_)], w=[ps2k])
                        vo(lambda h, ps2_=ps2_, sl=sl, h_=h_: h.tensor_tensor(
                            spair[:, h_ * 64:(h_ + 1) * 64], ps2_[:, 0:64],
                            css[:, sl * 64:(sl + 1) * 64], ALU.add),
                           r=[ps2k, ("css", sl)], w=[("sp", h_)])
                        OP = o_pair[:, h_ * 64:(h_ + 1) * 64]
                        vo(lambda h, po2=po2, OP=OP, sl=sl, h_=h_: h.scalar_tensor_tensor(
                            OP, po2[:, 0:64], dk(NQ8, h_, 1), p1s[:, sl * 64:(sl + 1) * 64],
                            ALU.mult, ALU.add),
                           r=[po2k, DKK, ("p1s", sl)],
                           w=[("op", h_)] + ([("wnat",)] if g == 0 and c == 0 and h_ == 0 else []))
                        so(lambda h, OP=OP: h.activation(
                            q_sg[:, 0:64], OP, AF.Square),
                           r=[("op", h_), ("qsg",)], w=[("qsg",)])
                        vo(lambda h, h_=h_: h.tensor_reduce(
                            dk(RR, h_, 1), q_sg[:, 0:64], AX.X, ALU.add),
                           r=[("qsg",)], w=[("rr", h_)])
            LBL[0] = 'oscale'
            vo(lambda h: h.tensor_scalar(dk(SC1), dk(RR), 1.0 / 64.0, None, ALU.mult),
               r=[("rr", hh2) for hh2 in range(16)] + [DKK], w=[DKK])
            so(lambda h: h.activation(dk(SC1), dk(SC1), AF.Sqrt, bias=epson[:]),
               r=[DKK, KC], w=[DKK])
            vo(lambda h: h.reciprocal(dk(OSC), dk(SC1)), r=[DKK], w=[DKK])
            for h_ in range(16):
                vo(lambda h, h_=h_: h.tensor_scalar(
                    o_pair[:, h_ * 64:(h_ + 1) * 64], o_pair[:, h_ * 64:(h_ + 1) * 64],
                    dk(OSC, h_, 1), None, ALU.mult), r=[DKK, ("op", h_)], w=[("op", h_)])
            for hp in range(8):
                pt2, ptk2 = rot_tr.get(128)
                tr(pt2, o_pair[:, hp * 128:(hp + 1) * 128], id128[:],
                   r=[("op", hp * 2), ("op", hp * 2 + 1), KC], w=[ptk2])
                so(lambda h, pt2=pt2, hp=hp, c=c: h.copy(
                    pair(oT, hp * GC, c), pt2[:].rearrange("p (b r) -> p b r", b=2)),
                   r=[ptk2], w=[("oT", hp, c)])

        if g == 0: t.mark['g0_delta'] = len(t.sched)

        LBL[0] = 'wo'
        for b in range(2):
            for half in range(2):
                pg, pgk = rot_big.get(512)
                for ic in range(8):
                    mm(pg, oT[:, ic * GC + b * 128: ic * GC + b * 128 + 128],
                       woT[:, ic * 1024 + half * 512: ic * 1024 + (half + 1) * 512],
                       ic == 0, ic == 7,
                       r=[("oT", ic, 0), ("oT", ic, 1)] + [("Wo", ic, o2) for o2 in range(8)],
                       w=[pgk])
                vo(lambda h, pg=pg, b=b, half=half, XT=XT: h.tensor_tensor(
                    x2[:, b * H + half * 512: b * H + (half + 1) * 512], pg,
                    XT[:, b * H + half * 512: b * H + (half + 1) * 512], ALU.add),
                   r=[pgk, XTK[b]], w=[("x2", b)])

        LBL[0] = 'ln2q'
        for b in range(2):
            emit_ln(("x2", b), x2[:, b * H:(b + 1) * H], b,
                    fstage[:, 0:H], ("fst",))
            quant(fstage[:, 0:H], xn_tm[:, b * H:(b + 1) * H], H,
                  [("fst",)], ("xn", b))
            bank, bkk = pbB[b], ("psb", 6 + b)
            for ic in range(8):
                tr(bank[:, ic * 128:(ic + 1) * 128],
                   xn_tm[:, b * H + ic * 128: b * H + (ic + 1) * 128], id128[:],
                   r=[("xn", b), KC], w=[bkk])
            so(lambda h, bank=bank, b=b: h.copy(
                xqT[:, :].rearrange("p (i g2) -> p i g2", g2=GC)[:, :, b * 128:(b + 1) * 128],
                bank[:, 0:1024].rearrange("p (i t) -> p i t", t=128)),
               r=[bkk], w=[("xqT", ic2, b) for ic2 in range(8)])

        if g == 0: t.mark['g0_ln2'] = len(t.sched)
        LBL[0] = 'ffngu'
        for oc4 in range(8):
            pr = oc4 % 2
            dma("SY", wgs[pr][:, :].rearrange("p (i c) -> p i c", c=352),
                wgT_h.rearrange("p (i c) -> p i c", c=IF)[:, :, oc4 * 352:(oc4 + 1) * 352],
                r=[("wgT_s", t2) for t2 in range(22)] if g == 0 and oc4 == 0 else [],
                w=[("wgs", pr)])
            dma("SY", wus[pr][:, :].rearrange("p (i c) -> p i c", c=352),
                wuT_h.rearrange("p (i c) -> p i c", c=IF)[:, :, oc4 * 352:(oc4 + 1) * 352],
                r=[("wuT_s", t2) for t2 in range(22)] if g == 0 and oc4 == 0 else [],
                w=[("wus", pr)])
            for b in range(2):
                pgt, pgtk = rot_big.get(352)
                put, putk = rot_big.get(352)
                for ic in range(8):
                    mm(pgt, xqT[:, ic * GC + b * 128: ic * GC + b * 128 + 128],
                       wgs[pr][:, ic * 352:(ic + 1) * 352], ic == 0, ic == 7,
                       r=[("xqT", ic, b), ("wgs", pr)], w=[pgtk])
                for ic in range(8):
                    mm(put, xqT[:, ic * GC + b * 128: ic * GC + b * 128 + 128],
                       wus[pr][:, ic * 352:(ic + 1) * 352], ic == 0, ic == 7,
                       r=[("xqT", ic, b), ("wus", pr)], w=[putk])
                so(lambda h, pgt=pgt: h.activation(fstage[:, 0:352], pgt, AF.Silu),
                   r=[pgtk, ("fst",)], w=[("fst",)])
                vo(lambda h, put=put: h.tensor_tensor(
                    fstage[:, 352:704], fstage[:, 0:352], put, ALU.mult),
                   r=[putk, ("fst",)], w=[("fst",)])
                hwk = [("hq", b, oc4)]
                if g == 0 and b == 0 and oc4 == 0:
                    hwk += [("qtile",), ("qtile", 0), ("qtile", 1), ("qtile", 2)]
                quant(fstage[:, 352:704],
                      h_tm[:, b * IF + oc4 * 352: b * IF + (oc4 + 1) * 352], 352,
                      [("fst",)], hwk)
        LBL[0] = 'hqtr'
        for b in range(2):
            for grp in range(3):
                k0 = grp * 8
                kn = min(8, 22 - k0)
                bank, bkk = pbB[grp % 2], ("psb", 6 + grp % 2)
                hkeys = set()
                for j in range(kn):
                    kc = k0 + j
                    p0, p1 = (kc * 128) // 352, (kc * 128 + 127) // 352
                    hkeys.add(p0); hkeys.add(p1)
                    tr(bank[:, j * 128:(j + 1) * 128],
                       h_tm[:, b * IF + kc * 128: b * IF + (kc + 1) * 128], id128[:],
                       r=[("hq", b, p0), ("hq", b, p1), KC], w=[bkk])
                so(lambda h, bank=bank, b=b, k0=k0, kn=kn: h.copy(
                    hqT[:, b * 22 * 128 + k0 * 128: b * 22 * 128 + (k0 + kn) * 128],
                    bank[:, 0:kn * 128]),
                   r=[bkk], w=[("hqT", b, kc2) for kc2 in range(k0, k0 + kn)] + (
                       [("stg", k2) for k2 in range(22)]
                       if (g == 0 and b == 0 and grp == 0) else []))
        LBL[0] = 'down'
        pd = [pbF[i][:, 0:512] for i in (0, 1, 4, 5)]
        pdk = [BK[i] for i in (0, 1, 4, 5)]
        for kc in range(22):
            pr = kc % 2
            dma("SY", wds[pr][:, :], wdT_h[:, kc * 1024:(kc + 1) * 1024],
                r=[("wdT_s", o2) for o2 in range(8)] if g == 0 and kc == 0 else [],
                w=[("wds", pr)])
            for b in range(2):
                for half in range(2):
                    mm(pd[b * 2 + half],
                       hqT[:, b * 22 * 128 + kc * 128: b * 22 * 128 + (kc + 1) * 128],
                       wds[pr][:, half * 512:(half + 1) * 512], kc == 0, kc == 21,
                       r=[("hqT", b, kc), ("wds", pr)], w=[pdk[b * 2 + half]])
        LBL[0] = 'outst'
        for b in range(2):
            for half in range(2):
                vo(lambda h, b=b, half=half: h.tensor_tensor(
                    out_t[:, 512:1024], pd[b * 2 + half],
                    x2[:, b * H + half * 512: b * H + (half + 1) * 512], ALU.add),
                   r=[pdk[b * 2 + half], ("x2", b), ("fst",)], w=[("fst",)])
                dma("GP", out_h[b, g * 128:(g + 1) * 128, half * 512:(half + 1) * 512],
                    out_t[:, 512:1024], r=[("fst",)], w=[("outd", g, b, half)])
        if DBG and g == int(os.environ.get('BASS_DBG_G', 0)):
            dma("GP", dbg_h[4, :, 0:2048], x2[:, 0:2048], r=[("x2", 0), ("x2", 1)], w=[("dbgd", 4)])
            dma("GP", dbg_h[1, :, 0:2816], h_tm[:, 0:IF],
                r=[("hq", 0, o4) for o4 in range(8)], w=[("dbgd", 1)])
            dma("GP", dbg_h[2, :, 0:1024], xn_tm[:, 0:H], r=[("xn", 0)], w=[("dbgd", 2)])

    trunc = os.environ.get("BASS_TRUNC")
    if trunc:
        n = t.mark[trunc] if trunc in t.mark else int(trunc)
        t.sched = t.sched[:n]
        t.cnt = dict.fromkeys(ENGS, 0)
        for eng_, fn_, w_, c_, p_ in t.sched:
            t.cnt[eng_] = max(t.cnt[eng_], c_)
        t.lw = {k: v for k, v in t.lw.items() if v[1] <= t.cnt[v[0]]}
        t.rd = {k: {e2: c2 for e2, c2 in d2.items() if c2 <= t.cnt[e2]}
                for k, d2 in t.rd.items()}
        print("TRUNC at", trunc, n, "marks:", t.mark)
        vo(lambda h: h.memset(fstage[:], 0.0), w=[("fst",)])
        for b_ in range(2):
            for r_ in range(16):
                dma("GP", out_h[b_, r_ * 128:(r_ + 1) * 128, :], fstage[:],
                    r=[("fst",)], w=[("outd", b_, r_)])
    t.op("VE", lambda h: h.nop(), xwaits={"GP": t.cnt["GP"], "SY": t.cnt["SY"]})

    per, remap, waited_dma = t.emit_plan()
    lp = nc.allow_low_precision(reason="bf16 elementwise sanctioned by problem tolerance")
    lp.__enter__()
    with (nc.semaphore("sTE") as sTE, nc.semaphore("sVE") as sVE,
          nc.semaphore("sSE") as sSE, nc.semaphore("sGP") as sGP,
          nc.semaphore("sSY") as sSY, nc.Block() as block):
        sems = {"TE": sTE, "VE": sVE, "SE": sSE, "GP": sGP, "SY": sSY}

        def emit_for(e, h):
            import kernel as _k
            dbg = getattr(_k, "_EMIT_LOG", None)
            dma_q = e in ("GP", "SY")
            k = 0
            for fn, waits, inc, par in per[e]:
                for e2, c2 in sorted(waits.items()):
                    h.wait_ge(sems[e2], remap(e2, c2))
                PAR_ACT[0] = par
                inst = fn(h)
                if dbg is not None:
                    try:
                        dbg.append((getattr(inst.ins, "name", None), e, waits))
                    except Exception:
                        pass
                if inc:
                    inst.then_inc(sems[e], 16 if dma_q else 1)
                if dma_q:
                    k += 1
                    if waited_dma[e] and k in waited_dma[e]:
                        h.wait_ge(sems[e], 16 * k)

        @block.tensor
        def _(eng):
            emit_for("TE", eng)

        @block.vector
        def _(eng):
            emit_for("VE", eng)

        @block.scalar
        def _(eng):
            emit_for("SE", eng)

        @block.gpsimd
        def _(eng):
            emit_for("GP", eng)

        @block.sync
        def _(eng):
            emit_for("SY", eng)

    lp.__exit__(None, None, None)
    for ccc in ctxs[::-1]:
        ccc.__exit__(None, None, None)
    build_program.last_t = t
    return nc


_CACHED = {}


def _consts():
    pidx = np.arange(128)[:, None] % 64
    j = np.arange(64)[None, :]
    return {
        "id128": np.eye(128, dtype=ml_dtypes.bfloat16),
        "su_stack": (j > pidx).astype(ml_dtypes.bfloat16),
        "sl_stack": (j < pidx).astype(ml_dtypes.bfloat16),
        "ui_stack": (j >= pidx).astype(ml_dtypes.bfloat16),
        "id_stack": (j == pidx).astype(ml_dtypes.bfloat16),
        "bdones": np.concatenate(
            [np.repeat([[1.0, 0.0]], 64, 0), np.repeat([[0.0, 1.0]], 64, 0)]
        ).astype(ml_dtypes.bfloat16),
    }


def kernel(L0_attn_x, L0_attn_Wq, L0_attn_Wk, L0_attn_Wv, L0_attn_Wb, L0_attn_Wo,
           L0_attn_conv_q, L0_attn_conv_k, L0_attn_conv_v, L0_attn_o_norm_w,
           L0_ffn_w_gate, L0_ffn_w_up, L0_ffn_w_down):
    if "nc" not in _CACHED:
        _CACHED["nc"] = build_program()
    nc = _CACHED["nc"]
    cst = _consts()
    shared = dict(
        Wq=np.ascontiguousarray(L0_attn_Wq), Wk=np.ascontiguousarray(L0_attn_Wk),
        Wv=np.ascontiguousarray(L0_attn_Wv), Wb=np.ascontiguousarray(L0_attn_Wb),
        Wo=np.ascontiguousarray(L0_attn_Wo),
        conv_q=np.ascontiguousarray(L0_attn_conv_q),
        conv_k=np.ascontiguousarray(L0_attn_conv_k),
        conv_v=np.ascontiguousarray(L0_attn_conv_v),
        o_norm_w=np.ascontiguousarray(L0_attn_o_norm_w),
        w_gate=np.ascontiguousarray(L0_ffn_w_gate),
        w_up=np.ascontiguousarray(L0_ffn_w_up),
        w_down=np.ascontiguousarray(L0_ffn_w_down),
        **cst,
    )
    in_maps = []
    for i in range(8):
        m = dict(shared)
        m["x"] = np.ascontiguousarray(L0_attn_x[2 * i:2 * i + 2])
        in_maps.append(m)
    res = run_bass_kernel_spmd(nc, in_maps, core_ids=list(range(8)))
    return np.concatenate([r["out"] for r in res.results], axis=0)


# version 20 (final)
# speedup vs baseline: 1.0317x; 1.0317x over previous
import os
import numpy as np
import ml_dtypes
import concourse.bass as bass
import concourse.mybir as mybir
from concourse.bass_utils import run_bass_kernel_spmd

F32 = mybir.dt.float32
BF16 = mybir.dt.bfloat16
AF = mybir.ActivationFunctionType
ALU = mybir.AluOpType
AX = mybir.AxisListType

H = 1024
NH = 16
DH = 64
IF = 2816
CK = 4
T = 2048
GC = 256
NG = int(os.environ.get("BASS_NG", T // 128))
NCH = int(os.environ.get("BASS_NCH", 2))

ENGS = ("TE", "VE", "SE", "GP", "SY")


PAR = [0]
PAR_ACT = [0]
LBL = ["prep"]


class Trk:
    def __init__(self):
        self.sched = []
        self.cnt = dict.fromkeys(ENGS, 0)
        self.lw = {}
        self.rd = {}

    def op(self, eng, fn, r=(), w=(), xwaits=None):
        waits = dict(xwaits or {})

        def need(e2, c2):
            if e2 == eng:
                if eng in ("TE", "GP", "SY"):
                    return
                if self.cnt[eng] + 1 - c2 >= 6:
                    return
                if waits.get(e2, 0) < c2:
                    waits[e2] = c2
                return
            if e2 in ("GP", "SY"):
                c2 = self.cnt[e2]
            if waits.get(e2, 0) < c2:
                waits[e2] = c2

        for k in r:
            x = self.lw.get(k)
            if x:
                need(*x)
        for k in w:
            x = self.lw.get(k)
            if x:
                need(*x)
            for e2, c2 in self.rd.get(k, {}).items():
                need(e2, c2)
        self.cnt[eng] += 1
        c = self.cnt[eng]
        self.sched.append((eng, fn, waits, c, PAR[0]))
        self.lbl = getattr(self, 'lbl', []); self.lbl.append(LBL[0])
        for k in r:
            self.rd.setdefault(k, {})[eng] = c
        for k in w:
            self.lw[k] = (eng, c)
            self.rd[k] = {}
        return c

    def emit_plan(self):
        waited = {e: set() for e in ENGS}
        for eng, fn, waits, c, par in self.sched:
            for e2, c2 in waits.items():
                waited[e2].add(c2)
        semval = {}
        for e in ENGS:
            if e in ("GP", "SY"):
                continue
            acc, n = {}, 0
            for i in sorted(waited[e]):
                n += 1
                acc[i] = n
            semval[e] = acc

        def remap(e, c):
            return 16 * c if e in ("GP", "SY") else semval[e][c]

        per = {e: [] for e in ENGS}
        for eng, fn, waits, c, par in self.sched:
            inc = (eng in ("GP", "SY")) or (c in waited[eng])
            per[eng].append((fn, waits, inc, par))
        return per, remap, {e: waited[e] for e in ("GP", "SY")}


def build_program():
    nc = bass.Bass("TRN2", target_bir_lowering=False, detect_race_conditions=False)
    t = Trk()

    x_h = nc.dram_tensor("x", [2, T, H], F32, kind="ExternalInput").ap()
    w_h = {n: nc.dram_tensor(n, [H, H], F32, kind="ExternalInput").ap()
           for n in ("Wq", "Wk", "Wv", "Wo")}
    wb_h = nc.dram_tensor("Wb", [NH, H], F32, kind="ExternalInput").ap()
    cw_h = [nc.dram_tensor(n, [H, CK], F32, kind="ExternalInput").ap()
            for n in ("conv_q", "conv_k", "conv_v")]
    onw_h = nc.dram_tensor("o_norm_w", [DH], F32, kind="ExternalInput").ap()
    wg_h = nc.dram_tensor("w_gate", [IF, H], F32, kind="ExternalInput").ap()
    wu_h = nc.dram_tensor("w_up", [IF, H], F32, kind="ExternalInput").ap()
    wd_h = nc.dram_tensor("w_down", [H, IF], F32, kind="ExternalInput").ap()
    id128_h = nc.dram_tensor("id128", [128, 128], BF16, kind="ExternalInput").ap()
    su_h = nc.dram_tensor("su_stack", [128, 64], BF16, kind="ExternalInput").ap()
    sl_h = nc.dram_tensor("sl_stack", [128, 64], BF16, kind="ExternalInput").ap()
    ui_h = nc.dram_tensor("ui_stack", [128, 64], BF16, kind="ExternalInput").ap()
    ids_h = nc.dram_tensor("id_stack", [128, 64], BF16, kind="ExternalInput").ap()
    bdo_h = nc.dram_tensor("bdones", [128, 2], BF16, kind="ExternalInput").ap()
    out_h = nc.dram_tensor("out", [2, T, H], F32, kind="ExternalOutput").ap()
    DBG = os.environ.get("BASS_DBG") == "1"
    dbg_h = nc.dram_tensor("dbg", [6, 128, 2816], F32, kind="ExternalOutput").ap() if DBG else None
    wgT_h = nc.dram_tensor("wgT_s", [128, 8 * IF], BF16, kind="Internal").ap()
    wuT_h = nc.dram_tensor("wuT_s", [128, 8 * IF], BF16, kind="Internal").ap()
    wdT_h = nc.dram_tensor("wdT_s", [128, 22 * H], BF16, kind="Internal").ap()

    ctxs = []

    def sb(name, shape, dtype):
        c = nc.sbuf_tensor(name, shape, dtype)
        h = c.__enter__()
        ctxs.append(c)
        return h

    def psb(name, shape, dtype):
        c = nc.psum_tensor(name, shape, dtype)
        h = c.__enter__()
        ctxs.append(c)
        return h

    pbF = [psb(f"pbF{i}", [128, 512], F32) for i in range(6)]
    pbB = [psb(f"pbB{i}", [128, 1024], BF16) for i in range(2)]

    class Rot:

        def __init__(self, banks, bids, ncol, per):
            self.regs = []
            for i in range(per):
                for b, bid in zip(banks, bids):
                    self.regs.append((b, i * ncol, ("psb", bid)))
            self.i = 0

        def get(self, width):
            b, c0, k = self.regs[self.i % len(self.regs)]
            self.i += 1
            return b[:, c0:c0 + width], k

    rot_tr = Rot(pbB[0:2], (6, 7), 128, 1)
    rot_sol = Rot(pbF[2:6], (2, 3, 4, 5), 256, 1)
    rot_sm = Rot(pbF[0:2], (0, 1), 512, 1)
    rot_big = Rot(pbF[0:4], (0, 1, 2, 3), 512, 1)
    BK = [("psb", i) for i in range(8)]

    id128 = sb("id128_s", [128, 128], BF16)
    su = sb("su_s", [128, 64], BF16)
    slm = sb("slm", [128, 64], BF16)
    ui = sb("ui", [128, 64], BF16)
    ids = sb("ids", [128, 64], BF16)
    bdo = sb("bdo", [128, 2], BF16)
    onorm = sb("onorm", [128, 1], F32)
    zerop = sb("zerop", [128, 1], F32)
    epsln = sb("epsln", [128, 1], F32)
    epson = sb("epson", [128, 1], F32)

    wT = {n: sb(n + "T", [128, 64 * 128], BF16) for n in ("Wq", "Wk", "Wv")}
    woT = sb("woT", [128, 8 * 1024], BF16)
    wbT = sb("wbT", [128, 8 * 16], BF16)
    convw = sb("convw", [128, 3 * 32], F32)

    fstage = sb("fstage", [128, 1024], F32)

    q_abs = sb("q_abs", [128, 1024], F32)
    q_y = sb("q_y", [128, 1024], F32)
    q_a = sb("q_a", [128, 1024], BF16)
    q_b = sb("q_b", [128, 1024], BF16)
    q_sg = sb("q_sg", [128, 1024], BF16)
    q_gm = sb("q_gm", [128, 64], F32)
    q_sm = sb("q_sm", [128, 64], F32)
    q_in = sb("q_in", [128, 64], F32)


    xt = [sb(f"xt{i}", [128, 2 * H], F32) for i in range(2)]
    sums = sb("sums", [128, 4], F32)
    lnt = sb("lnt", [128, 12], F32)
    xn_tm = sb("xn_tm", [128, 2 * H], BF16)
    xnT = sb("xnT", [128, 8 * GC], BF16)
    qkv = sb("qkv", [128, 24 * GC], BF16)
    rawc = [sb(f"rawc{i}", [128, 262], BF16) for i in range(2)]
    halo = sb("halo", [128, 24 * 6], BF16)
    beta_p = sb("beta_p", [128, NCH * 16], F32)
    dkv = sb("dkv", [128, 2 * 8 * 16], F32)
    NKC, NU, DD, ND, NQ8, RR, OSC, SC1 = range(8)

    def dk(which, h0=0, n=16):
        o = PAR_ACT[0] * 128 + which * 16 + h0
        return dkv[:, o: o + n]

    maskD = sb("maskD", [128, 8 * 64], BF16)
    mqt = sb("mqt", [128, 8 * 64], BF16)
    zst = sb("zst", [128, 8 * 512], BF16)
    ktm = sb("ktm", [128, 8 * 64], BF16)
    ff = sb("ff", [128, 8 * 128], BF16)
    mst = sb("mst", [128, 8 * 64], BF16)
    qmp = sb("qmp", [128, 8 * 64], BF16)
    p1s = sb("p1s", [128, 8 * 64], BF16)
    css = sb("css", [128, 8 * 64], BF16)
    spair = sb("spair", [128, 16 * 64], BF16)
    o_pair = sb("o_pair", [128, 16 * 64], BF16)
    wnat = o_pair
    wbnat = spair[0:16, :]
    oT = sb("oT", [128, 8 * GC], BF16)
    x2 = sb("x2", [128, 2 * H], F32)
    xqT = sb("xqT", [128, 8 * GC], BF16)
    h_tm = sb("h_tm", [128, 2 * IF], BF16)
    hqT = sb("hqT", [128, 2 * 22 * 128], BF16)
    stg = hqT
    qtile = h_tm
    wgs = [sb("wgs0", [128, 8 * 352], BF16), sb("wgs1", [128, 8 * 352], BF16)]
    wus = [sb("wus0", [128, 8 * 352], BF16), sb("wus1", [128, 8 * 352], BF16)]
    wds = [sb("wds0", [128, 1024], BF16), sb("wds1", [128, 1024], BF16)]
    out_t = fstage

    def dma(q, dst, src, r=(), w=()):
        return t.op(q, lambda h, dst=dst, src=src: h.dma_start(dst, src), r=r, w=w)

    def mm(outp, lhsT, rhs, start, stop, tp=None, r=(), w=()):
        return t.op("TE", lambda h, o=outp, l=lhsT, rr=rhs, s1=start, s2=stop, tp=tp:
                    h.matmul(o, l, rr, start=s1, stop=s2, tile_position=tp), r=r, w=w)

    def tr(outp, inp, idt, r=(), w=()):
        return t.op("TE", lambda h, o=outp, i=inp, d=idt: h.transpose(o, i, d), r=r, w=w)

    def vo(fn, r=(), w=()):
        return t.op("VE", fn, r=r, w=w)

    def so(fn, r=(), w=()):
        return t.op("SE", fn, r=r, w=w)

    def go(fn, r=(), w=()):
        return t.op("GP", fn, r=r, w=w)

    KC = ("const",)

    dma("SY", id128[:], id128_h, w=[KC])
    dma("SY", su[:], su_h, w=[KC])
    dma("SY", slm[:], sl_h, w=[KC])
    dma("SY", ui[:], ui_h, w=[KC])
    dma("SY", ids[:], ids_h, w=[KC])
    dma("SY", bdo[:], bdo_h, w=[KC])
    dma("SY", onorm[0:64, :], onw_h.rearrange("(a b) -> a b", b=1), w=[KC])
    dma("SY", onorm[64:128, :], onw_h.rearrange("(a b) -> a b", b=1), w=[KC])
    vo(lambda h: h.memset(zerop[:], 0.0), w=[KC])
    vo(lambda h: h.memset(epsln[:], 1e-5), w=[KC])
    vo(lambda h: h.memset(epson[:], 1e-6), w=[KC])

    vo(lambda h: h.memset(halo[:], 0.0),
       w=[("halo", p, oc) for p in range(3) for oc in range(8)])

    for p in range(3):
        dma("GP", convw[:, p * 32:(p + 1) * 32].rearrange("p (c k) -> p c k", k=CK),
            cw_h[p].rearrange("(c p) k -> p c k", p=128), w=[KC])
    dma("GP", wbnat[:], wb_h, w=[("wbnat",)])

    def prep_attn(name, dstT, fold):
        for oc in range(8):
            dma("GP", wnat[:], w_h[name][oc * 128:(oc + 1) * 128, :], w=[("wnat",)])
            if True:
                for ic in range(8):
                    pt, pk = rot_tr.get(128)
                    tr(pt, wnat[:, ic * 128:(ic + 1) * 128],
                       id128[:], r=[("wnat",), KC], w=[pk])
                    if fold:
                        col = dstT[:, ic * 1024 + oc * 128: ic * 1024 + (oc + 1) * 128]
                        so(lambda h, pt=pt, col=col: h.activation(
                            col, pt, AF.Identity, bias=zerop[:], scale=onorm[:]),
                            r=[pk, KC], w=[(name, ic, oc)])
                    else:
                        col = dstT[:, (ic * 8 + oc) * 128: (ic * 8 + oc + 1) * 128]
                        so(lambda h, pt=pt, col=col: h.copy(col, pt),
                           r=[pk], w=[(name, ic, oc)])

    prep_attn("Wq", wT["Wq"], False)
    prep_attn("Wk", wT["Wk"], False)
    prep_attn("Wv", wT["Wv"], False)
    prep_attn("Wo", woT, True)

    for ic in range(8):
        pt, pk = rot_tr.get(16)
        tr(pt, wbnat[0:16, ic * 128:(ic + 1) * 128], id128[0:16, 0:16],
           r=[("wbnat",), KC], w=[pk])
        so(lambda h, pt=pt, ic=ic: h.copy(wbT[:, ic * 16:(ic + 1) * 16], pt),
           r=[pk], w=[("wbT", ic)])
    vo(lambda h: h.memset(spair[:], 0.0),
       w=[("sp", hh) for hh in range(16)] + [("wbnat",)])

    def quant(src, dst, width, rkeys, wkey):
        QS = ("qscr",)
        r = list(rkeys)
        nb = width // 16
        so(lambda h: h.activation(q_abs[:, :width], src, AF.Abs), r=r + [QS], w=[QS])
        so(lambda h: h.sign(q_sg[:, :width], src), r=r + [("qsg",)], w=[("qsg",)])
        vo(lambda h: h.tensor_reduce(
            q_gm[:, :nb].rearrange("p (g o) -> p g o", o=1),
            q_abs[:, :width].rearrange("p (g t) -> p g t", t=16), AX.X, ALU.max),
           r=[QS], w=[QS])
        vo(lambda h: h.tensor_scalar(q_sm[:, :nb], q_gm[:, :nb], 1.0 / 6.0, 1e-12, ALU.mult, ALU.max), r=[QS], w=[QS])
        vo(lambda h: h.reciprocal(q_in[:, :nb], q_sm[:, :nb]), r=[QS], w=[QS])
        invb = q_in[:, :nb].rearrange("p (g o) -> p g o", o=1).broadcast_to((128, nb, 16))
        smb = q_sm[:, :nb].rearrange("p (g o) -> p g o", o=1).broadcast_to((128, nb, 16))
        ab = q_abs[:, :width].rearrange("p (g t) -> p g t", t=16)
        yb = q_y[:, :width].rearrange("p (g t) -> p g t", t=16)
        vo(lambda h: h.tensor_tensor(yb, ab, invb, ALU.mult), r=[QS], w=[QS])
        y = q_y[:, :width]
        vo(lambda h: h.scalar_tensor_tensor(q_a[:, :width], y, 0.25, zerop[:, 0:1].broadcast_to((128, width)), ALU.is_gt, ALU.add), r=[QS, KC], w=[QS])
        for th in (0.75, 1.25, 1.75):
            vo(lambda h, th=th: h.scalar_tensor_tensor(q_a[:, :width], y, th, q_a[:, :width], ALU.is_gt, ALU.add), r=[QS], w=[QS])
        vo(lambda h: h.scalar_tensor_tensor(q_b[:, :width], y, 5.0, zerop[:, 0:1].broadcast_to((128, width)), ALU.is_gt, ALU.add), r=[QS, KC], w=[QS])
        vo(lambda h: h.scalar_tensor_tensor(q_b[:, :width], y, 5.0, q_b[:, :width], ALU.is_gt, ALU.add), r=[QS], w=[QS])
        for th in (2.5, 3.5):
            vo(lambda h, th=th: h.scalar_tensor_tensor(q_b[:, :width], y, th, q_b[:, :width], ALU.is_gt, ALU.add), r=[QS], w=[QS])
        vo(lambda h: h.scalar_tensor_tensor(q_a[:, :width], q_a[:, :width], 0.5, q_b[:, :width], ALU.mult, ALU.add), r=[QS], w=[QS])
        vo(lambda h: h.tensor_tensor(q_b[:, :width], q_a[:, :width], q_sg[:, :width], ALU.mult), r=[QS, ("qsg",)], w=[QS])
        qb2 = q_b[:, :width].rearrange("p (g t) -> p g t", t=16)
        db = dst.rearrange("p (g t) -> p g t", t=16)
        wk = wkey if isinstance(wkey, list) else [wkey]
        return vo(lambda h: h.tensor_tensor(db, qb2, smb, ALU.mult), r=[QS], w=wk)

    def prep_gu(hbm, dst_h, nm):
        for tt_ in range(22):
            dma("GP", fstage[:, :], hbm[tt_ * 128:(tt_ + 1) * 128, :],
                w=[("fst",)])
            quant(fstage[:, :], qtile[:, 0:1024], 1024, [("fst",)], ("qtile",))
            for ic in range(8):
                pt, pk = rot_tr.get(128)
                tr(pt, qtile[:, ic * 128:(ic + 1) * 128], id128[:],
                   r=[("qtile",), KC], w=[pk])
                so(lambda h, pt=pt, ic=ic: h.copy(stg[:, ic * 128:(ic + 1) * 128], pt),
                   r=[pk], w=[("stg", ic)])
            dma("SY", dst_h.rearrange("p (i c) -> p i c", c=IF)[:, :, tt_ * 128:(tt_ + 1) * 128],
                stg[:, 0:1024].rearrange("p (i c) -> p i c", c=128),
                r=[("stg", i2) for i2 in range(8)], w=[(nm, tt_)])

    t.mark = getattr(t, 'mark', {}); t.mark['prep_attn'] = len(t.sched)
    prep_gu(wg_h, wgT_h, "wgT_s")
    prep_gu(wu_h, wuT_h, "wuT_s")

    for oc in range(8):
        for piece, (c0, cw) in enumerate([(0, 1024), (1024, 1024), (2048, 768)]):
            dma("GP", fstage[:, 0:cw], wd_h[oc * 128:(oc + 1) * 128, c0:c0 + cw],
                w=[("fst",)])
            quant(fstage[:, 0:cw], qtile[:, c0:c0 + cw], cw,
                  [("fst",)], ("qtile", piece))
        for kc in range(22):
            pt, pk = rot_tr.get(128)
            tr(pt, qtile[:, kc * 128:(kc + 1) * 128], id128[:],
               r=[("qtile", 0), ("qtile", 1), ("qtile", 2), KC], w=[pk])
            so(lambda h, pt=pt, kc=kc: h.copy(stg[:, kc * 128:(kc + 1) * 128], pt),
               r=[pk], w=[("stg", kc)])
        dma("SY", wdT_h.rearrange("p (k c) -> p k c", c=H)[:, :, oc * 128:(oc + 1) * 128],
            stg[:, 0:IF].rearrange("p (k c) -> p k c", c=128),
            r=[("stg", k2) for k2 in range(22)], w=[("wdT_s", oc)])

    t.mark['prep'] = len(t.sched)
    WNM = ("Wq", "Wk", "Wv")

    def pair(buf, base, c):
        return buf[:, base:base + GC].rearrange("p (b r) -> p b r", b=2)[:, :, c * 64:(c + 1) * 64]

    def emit_ln(xs_key, xs, b, dst, dst_key):
        vo(lambda h: h.tensor_reduce(sums[:, b * 2:b * 2 + 1], xs, AX.X, ALU.add),
           r=[xs_key], w=[("sums", b)])
        so(lambda h: h.activation(q_abs[:, 0:H], xs, AF.Square),
           r=[xs_key, ("qscr",)], w=[("qscr",)])
        vo(lambda h: h.tensor_reduce(sums[:, b * 2 + 1:b * 2 + 2], q_abs[:, 0:H], AX.X, ALU.add),
           r=[("qscr",)], w=[("sums2", b)])
        L = lnt[:, b * 6:(b + 1) * 6]
        LK = ("lnt", b)
        vo(lambda h: h.tensor_scalar(L[:, 0:1], sums[:, b * 2:b * 2 + 1], 1.0 / H, None, ALU.mult),
           r=[("sums", b)], w=[LK])
        vo(lambda h: h.tensor_scalar(L[:, 1:2], sums[:, b * 2 + 1:b * 2 + 2], 1.0 / H, None, ALU.mult),
           r=[("sums2", b)], w=[LK])
        vo(lambda h: h.tensor_tensor(L[:, 2:3], L[:, 0:1], L[:, 0:1], ALU.mult), r=[LK], w=[LK])
        vo(lambda h: h.tensor_tensor(L[:, 2:3], L[:, 1:2], L[:, 2:3], ALU.subtract), r=[LK], w=[LK])
        so(lambda h: h.activation(L[:, 3:4], L[:, 2:3], AF.Sqrt, bias=epsln[:], scale=zerop[:].broadcast_to((128,1)) if False else 1.0), r=[LK, KC], w=[LK])
        vo(lambda h: h.reciprocal(L[:, 4:5], L[:, 3:4]), r=[LK], w=[LK])
        vo(lambda h: h.scalar_tensor_tensor(L[:, 5:6], L[:, 0:1], -1.0, L[:, 4:5], ALU.mult, ALU.mult),
           r=[LK], w=[LK])
        so(lambda h: h.activation(dst, xs, AF.Identity, bias=L[:, 5:6], scale=L[:, 4:5]),
           r=[LK, xs_key], w=[dst_key])

    for b in range(2):
        dma("GP", xt[0][:, b * H:(b + 1) * H], x_h[b, 0:128, :], w=[("xtl", 0, b)])
    for g in range(NG):
        par = g % 2
        XT = xt[par]
        XTK = [("xtl", par, 0), ("xtl", par, 1)]
        LBL[0] = 'xln'
        if g + 1 < NG:
            npar = (g + 1) % 2
            for b in range(2):
                dma("GP", xt[npar][:, b * H:(b + 1) * H],
                    x_h[b, (g + 1) * 128:(g + 2) * 128, :], w=[("xtl", npar, b)])
        for b in range(2):
            emit_ln(XTK[b], XT[:, b * H:(b + 1) * H], b,
                    xn_tm[:, b * H:(b + 1) * H], ("xn", b))
        for b in range(2):
            bank, bkk = pbB[b], ("psb", 6 + b)
            for ic in range(8):
                tr(bank[:, ic * 128:(ic + 1) * 128],
                   xn_tm[:, b * H + ic * 128: b * H + (ic + 1) * 128], id128[:],
                   r=[("xn", b), KC], w=[bkk])
            so(lambda h, bank=bank, b=b: h.copy(
                xnT[:, :].rearrange("p (i g2) -> p i g2", g2=GC)[:, :, b * 128:(b + 1) * 128],
                bank[:, 0:1024].rearrange("p (i t) -> p i t", t=128)),
                r=[bkk], w=[("xnT", ic2, b) for ic2 in range(8)])
        if DBG and g == int(os.environ.get('BASS_DBG_G', 0)):
            dma("GP", dbg_h[0, :, 0:2048], XT[:, 0:2048], r=[XTK[0], XTK[1]], w=[("dbgd", 0)])
            dma("GP", dbg_h[2, :, 0:12], lnt[:, 0:12], r=[("lnt", 0), ("lnt", 1)], w=[("dbgd", 2)])
            dma("GP", dbg_h[3, :, 0:2048], xn_tm[:, 0:2048], r=[("xn", 0), ("xn", 1)], w=[("dbgd", 3)])
        LBL[0] = 'beta'
        for c in range(NCH):
            pbm, pbk = rot_sm.get(16)
            for b in range(2):
                for ic in range(8):
                    mm(pbm[b * 64:(b + 1) * 64, 0:16],
                       xnT[:, ic * GC + b * 128 + c * 64: ic * GC + b * 128 + c * 64 + 64],
                       wbT[:, ic * 16:(ic + 1) * 16], ic == 0, ic == 7, tp=(0, b * 64),
                       r=[("xnT", ic, b), ("wbT", ic)], w=[pbk])
            so(lambda h, pbm=pbm, c=c: h.activation(
                beta_p[:, c * 16:(c + 1) * 16], pbm, AF.Sigmoid),
                r=[pbk], w=[("beta", c)])
        LBL[0] = 'qkvconv'
        for p in range(3):
            for oc in range(8):
                pg, pgk = rot_big.get(GC)
                for ic in range(8):
                    mm(pg, wT[WNM[p]][:, (ic * 8 + oc) * 128:(ic * 8 + oc + 1) * 128],
                       xnT[:, ic * GC:(ic + 1) * GC], ic == 0, ic == 7,
                       r=[("xnT", ic, 0), ("xnT", ic, 1), (WNM[p], ic, oc)], w=[pgk])
                rb = rawc[oc % 2]
                RK = ("rawc", oc % 2)
                HK = ("halo", p, oc)
                so(lambda h, pg=pg, rb=rb: h.copy(rb[:, 3:131], pg[:, 0:128]), r=[pgk], w=[RK])
                so(lambda h, pg=pg, rb=rb: h.copy(rb[:, 134:262], pg[:, 128:256]), r=[pgk], w=[RK])
                vo(lambda h, rb=rb, p=p, oc=oc: h.tensor_copy(
                    rb[:, 0:262].rearrange("p (s r) -> p s r", s=2)[:, :, 0:3],
                    halo[:, (p * 8 + oc) * 6:(p * 8 + oc) * 6 + 6].rearrange("p (s r) -> p s r", s=2)),
                   r=[HK], w=[RK])
                rb2 = rb[:, 0:262].rearrange("p (s r) -> p s r", s=2)
                acc2 = q_y[:, 0:256].rearrange("p (s r) -> p s r", s=2)
                wv_ = [convw[:, p * 32 + oc * 4 + k: p * 32 + oc * 4 + k + 1] for k in range(CK)]
                vo(lambda h, rb2=rb2, acc2=acc2, wv_=wv_: h.tensor_scalar(
                    acc2, rb2[:, :, 0:128], wv_[0][:], None, ALU.mult),
                   r=[RK, KC, ("qscr",)], w=[("qscr",)])
                for k2 in (1, 2, 3):
                    vo(lambda h, rb2=rb2, acc2=acc2, wv_=wv_, k2=k2: h.scalar_tensor_tensor(
                        acc2, rb2[:, :, k2:k2 + 128], wv_[k2][:], acc2, ALU.mult, ALU.add),
                       r=[RK, KC, ("qscr",)], w=[("qscr",)])
                vo(lambda h, rb=rb, p=p, oc=oc: h.tensor_copy(
                    halo[:, (p * 8 + oc) * 6:(p * 8 + oc) * 6 + 6].rearrange("p (s r) -> p s r", s=2),
                    rb[:, 0:262].rearrange("p (s r) -> p s r", s=2)[:, :, 128:131]),
                   r=[RK], w=[HK])
                so(lambda h, p=p, oc=oc: h.activation(
                    qkv[:, (p * 8 + oc) * GC:(p * 8 + oc + 1) * GC], q_y[:, 0:256], AF.Silu),
                   r=[("qscr",)], w=[("qkv", p, oc)])

        if g == 0: t.mark['g0_attnpre'] = len(t.sched)

        for c in range(NCH):
            PAR[0] = (g * NCH + c) % 2
            LBL[0] = 'norm'
            pnq, pnqk = rot_sm.get(16)
            pnk, pnkk = rot_sm.get(16)
            for hp in range(8):
                st_, sp_ = hp == 0, hp == 7
                so(lambda h, hp=hp, c=c: h.activation(
                    q_a[:, hp * 128:(hp + 1) * 128].rearrange("p (b r) -> p b r", b=2),
                    pair(qkv, (0 * 8 + hp) * GC, c), AF.Square),
                   r=[("qkv", 0, hp), ("qscr",)], w=[("sqq", hp), ("qscr",)])
                so(lambda h, hp=hp, c=c: h.activation(
                    q_b[:, hp * 128:(hp + 1) * 128].rearrange("p (b r) -> p b r", b=2),
                    pair(qkv, (1 * 8 + hp) * GC, c), AF.Square),
                   r=[("qkv", 1, hp), ("qscr",)], w=[("sqk", hp), ("qscr",)])
                mm(pnq[:, hp * 2:hp * 2 + 2], q_a[:, hp * 128:(hp + 1) * 128], bdo[:],
                   st_, sp_, r=[("sqq", hp), KC], w=[pnqk])
                mm(pnk[:, hp * 2:hp * 2 + 2], q_b[:, hp * 128:(hp + 1) * 128], bdo[:],
                   st_, sp_, r=[("sqk", hp), KC], w=[pnkk])
            DKK = ("dkv", (g * NCH + c) % 2)
            so(lambda h, pnk=pnk: h.activation(dk(SC1), pnk, AF.Sqrt), r=[pnkk, DKK], w=[DKK])
            vo(lambda h: h.tensor_scalar(dk(NKC), dk(SC1), 1e-6, None, ALU.max), r=[DKK], w=[DKK])
            vo(lambda h: h.reciprocal(dk(NU), dk(NKC)), r=[DKK], w=[DKK])
            vo(lambda h: h.tensor_tensor(dk(DD), dk(NU), dk(NU), ALU.mult), r=[DKK], w=[DKK])
            vo(lambda h, c=c: h.tensor_tensor(dk(DD), dk(DD), beta_p[:, c * 16:(c + 1) * 16], ALU.mult),
               r=[DKK, ("beta", c)], w=[DKK])
            vo(lambda h: h.tensor_scalar(dk(ND), dk(DD), -1.0, None, ALU.mult), r=[DKK], w=[DKK])
            so(lambda h, pnq=pnq: h.activation(dk(SC1), pnq, AF.Sqrt), r=[pnqk, DKK], w=[DKK])
            vo(lambda h: h.tensor_scalar(dk(SC1), dk(SC1), 1e-6, None, ALU.max), r=[DKK], w=[DKK])
            vo(lambda h: h.reciprocal(dk(SC1), dk(SC1)), r=[DKK], w=[DKK])
            vo(lambda h: h.tensor_scalar(dk(NQ8), dk(SC1), 0.125, None, ALU.mult), r=[DKK], w=[DKK])

            for blk in range(2):
                LBL[0] = 'kvtr'
                heads = list(range(blk * 8, blk * 8 + 8))
                for hp in range(blk * 4, blk * 4 + 4):
                    pv, pvk = rot_tr.get(128)
                    pk_, pkk = rot_tr.get(128)
                    for b in range(2):
                        col = (2 * 8 + hp) * GC + b * 128 + c * 64
                        t.op("TE", lambda h, pv=pv, col=col, b=b: h.transpose(
                            pv[b * 64:(b + 1) * 64, :], qkv[:, col:col + 64], id128[:],
                            tile_position=(0, b * 64)),
                            r=[("qkv", 2, hp), KC], w=[pvk])
                        col2 = (1 * 8 + hp) * GC + b * 128 + c * 64
                        t.op("TE", lambda h, pk_=pk_, col2=col2, b=b: h.transpose(
                            pk_[b * 64:(b + 1) * 64, :], qkv[:, col2:col2 + 64], id128[:],
                            tile_position=(0, b * 64)),
                            r=[("qkv", 1, hp), KC], w=[pkk])
                    for h_ in (2 * hp, 2 * hp + 1):
                        sl = h_ % 8
                        Z = zst[:, sl * 512: sl * 512 + 512]
                        ZK = ("zst", sl)
                        hcol = (h_ % 2) * 64
                        vo(lambda h, pv=pv, Z=Z, hcol=hcol, h_=h_: h.tensor_scalar(
                            Z[:, 0:64], pv[:, hcol:hcol + 64], dk(NKC, h_, 1), None, ALU.mult),
                           r=[pvk, DKK], w=[ZK])
                        so(lambda h, pk_=pk_, Z=Z, hcol=hcol: h.copy(Z[:, 64:128], pk_[:, hcol:hcol + 64]),
                           r=[pkk], w=[ZK])
                        so(lambda h, pk_=pk_, hcol=hcol, sl=sl: h.copy(
                            ktm[:, sl * 64:(sl + 1) * 64], pk_[:, hcol:hcol + 64]),
                           r=[pkk], w=[("ktm", sl)])
                LBL[0] = 'gmask'
                ginfo = {}
                for h_ in heads:
                    hp, hh = h_ // 2, 64 * (h_ % 2)
                    sl = h_ % 8
                    Z = zst[:, sl * 512: sl * 512 + 512]
                    ZK = ("zst", sl)
                    pg_, pgk2 = rot_sm.get(128)
                    for b in range(2):
                        col = hp * GC + b * 128 + c * 64
                        lhs = qkv[hh:hh + 64, 1 * 8 * GC + col: 1 * 8 * GC + col + 64]
                        qsrc = qkv[hh:hh + 64, col: col + 64]
                        ksrc = qkv[hh:hh + 64, 1 * 8 * GC + col: 1 * 8 * GC + col + 64]
                        mm(pg_[b * 64:(b + 1) * 64, 0:64], lhs, qsrc, True, False,
                           tp=(hh, b * 64),
                           r=[("qkv", 0, hp), ("qkv", 1, hp)], w=[pgk2])
                        mm(pg_[b * 64:(b + 1) * 64, 64:128], lhs, ksrc, False, True,
                           tp=(hh, b * 64),
                           r=[("qkv", 1, hp)], w=[pgk2])
                    vo(lambda h, sl=sl, h_=h_: h.tensor_scalar(
                        maskD[:, sl * 64:(sl + 1) * 64], su[:], dk(ND, h_, 1), None, ALU.mult),
                       r=[KC, DKK], w=[("maskD", sl)])
                    vo(lambda h, pg_=pg_, Z=Z, sl=sl: h.tensor_tensor(
                        Z[:, 192:256], pg_[:, 64:128], maskD[:, sl * 64:(sl + 1) * 64], ALU.mult),
                       r=[pgk2, ("maskD", sl)], w=[ZK])
                    vo(lambda h, pg_=pg_, Z=Z: h.tensor_tensor(
                        Z[:, 128:192], pg_[:, 64:128], slm[:], ALU.mult),
                       r=[pgk2, KC], w=[ZK])
                    vo(lambda h, pg_=pg_, sl=sl: h.tensor_tensor(
                        mqt[:, sl * 64:(sl + 1) * 64], pg_[:, 0:64], ui[:], ALU.mult),
                       r=[pgk2, KC], w=[("mqt", sl)])
                    ginfo[h_] = (Z, ZK, sl)
                LBL[0] = 'solve'
                for i in range(6):
                    for h_ in heads:
                        Z, ZK, sl = ginfo[h_]
                        src = Z[:, (i % 2) * 256:(i % 2) * 256 + 256]
                        pa, pak = rot_sol.get(256)
                        wdt = 192 if i < 5 else 128
                        for b in range(2):
                            mm(pa[b * 64:(b + 1) * 64, 0:wdt],
                               src[b * 64:(b + 1) * 64, 192:256],
                               src[b * 64:(b + 1) * 64, 0:wdt],
                               True, i == 5, tp=(b * 64, b * 64), r=[ZK], w=[pak])
                            if i < 5:
                                mm(pa[b * 64:(b + 1) * 64, 192:256],
                                   src[b * 64:(b + 1) * 64, 128:192],
                                   src[b * 64:(b + 1) * 64, 192:256],
                                   False, True, tp=(b * 64, b * 64), r=[ZK], w=[pak])
                        if i < 5:
                            dstz = Z[:, ((i + 1) % 2) * 256:((i + 1) % 2) * 256 + 256]
                            vo(lambda h, pa=pa, src=src, dstz=dstz: h.tensor_tensor(
                                dstz[:, 0:128], pa[:, 0:128], src[:, 0:128], ALU.add),
                               r=[pak, ZK], w=[ZK])
                            vo(lambda h, pa=pa, dstz=dstz: h.tensor_copy(dstz[:, 128:192], pa[:, 128:192]),
                               r=[pak], w=[ZK])
                            so(lambda h, pa=pa, dstz=dstz, h_=h_: h.activation(
                                dstz[:, 192:256], pa[:, 192:256], AF.Identity,
                                bias=zerop[:], scale=dk(ND, h_, 1)),
                               r=[pak, DKK], w=[ZK])
                        else:
                            vo(lambda h, pa=pa, src=src, sl=sl: h.tensor_tensor(
                                ff[:, sl * 128:(sl + 1) * 128], pa[:, 0:128], src[:, 0:128], ALU.add),
                               r=[pak, ZK], w=[("ff", sl)])
                LBL[0] = 'pphase'
                for sub in range(2):
                    for h_ in heads[sub * 4:sub * 4 + 4]:
                        hp, hh = h_ // 2, 64 * (h_ % 2)
                        sl = h_ % 8
                        FK = ("ff", sl)
                        vo(lambda h, sl=sl, h_=h_: h.tensor_scalar(
                            ff[:, sl * 128:sl * 128 + 64], ff[:, sl * 128:sl * 128 + 64],
                            dk(DD, h_, 1), None, ALU.mult), r=[FK, DKK], w=[FK])
                        vo(lambda h, sl=sl, h_=h_: h.tensor_scalar(
                            ff[:, sl * 128 + 64:sl * 128 + 128], ff[:, sl * 128 + 64:sl * 128 + 128],
                            dk(DD, h_, 1), None, ALU.mult), r=[FK, DKK], w=[FK])
                        UV = ff[:, sl * 128:sl * 128 + 64]
                        WB = ff[:, sl * 128 + 64:sl * 128 + 128]
                        MQ = mqt[:, sl * 64:(sl + 1) * 64]
                        KT = ktm[:, sl * 64:(sl + 1) * 64]
                        SP = spair[:, h_ * 64:(h_ + 1) * 64]
                        psess, psk = rot_sm.get(256)
                        po, pok = psess[:, 0:64], psk
                        pc_, pck = psess[:, 64:128], psk
                        pp, ppk = psess[:, 128:192], psk
                        pm, pmk = psess[:, 192:256], psk
                        for b in range(2):
                            mm(po[b * 64:(b + 1) * 64, :], MQ[b * 64:(b + 1) * 64, :],
                               UV[b * 64:(b + 1) * 64, :], True, False, tp=(b * 64, b * 64),
                               r=[("mqt", sl), FK], w=[pok])
                            mm(pc_[b * 64:(b + 1) * 64, :], KT[b * 64:(b + 1) * 64, :],
                               UV[b * 64:(b + 1) * 64, :], False, False, tp=(b * 64, b * 64),
                               r=[("ktm", sl), FK], w=[pck])
                            mm(pp[b * 64:(b + 1) * 64, :], WB[b * 64:(b + 1) * 64, :],
                               MQ[b * 64:(b + 1) * 64, :], False, False, tp=(b * 64, b * 64),
                               r=[FK, ("mqt", sl)], w=[ppk])
                            mm(pm[b * 64:(b + 1) * 64, :], WB[b * 64:(b + 1) * 64, :],
                               KT[b * 64:(b + 1) * 64, :], False, True, tp=(b * 64, b * 64),
                               r=[FK, ("ktm", sl)], w=[pmk])
                        vo(lambda h, po=po, sl=sl, h_=h_: h.tensor_scalar(
                            p1s[:, sl * 64:(sl + 1) * 64], po[:, 0:64], dk(NQ8, h_, 1), None, ALU.mult),
                           r=[pok, DKK], w=[("p1s", sl)])
                        vo(lambda h, pc_=pc_, sl=sl: h.tensor_copy(css[:, sl * 64:(sl + 1) * 64], pc_[:, 0:64]),
                           r=[pck], w=[("css", sl)])
                        for b in range(2):
                            qsl = qkv[hh:hh + 64, hp * GC + b * 128 + c * 64:
                                      hp * GC + b * 128 + c * 64 + 64]
                            vo(lambda h, qsl=qsl, pp=pp, b=b, sl=sl: h.tensor_tensor(
                                qmp[b * 64:(b + 1) * 64, sl * 64:(sl + 1) * 64], qsl,
                                pp[b * 64:(b + 1) * 64, 0:64], ALU.subtract),
                               r=[ppk, ("qkv", 0, hp)], w=[("qmp", sl)])
                        vo(lambda h, pm=pm, sl=sl: h.scalar_tensor_tensor(
                            mst[:, sl * 64:(sl + 1) * 64], pm[:, 0:64], -1.0, ids[:], ALU.mult, ALU.add),
                           r=[pmk, KC], w=[("mst", sl)])
                        ssess, ssk = rot_sm.get(128)
                        po2, po2k = ssess[:, 0:64], ssk
                        ps2_, ps2k = ssess[:, 64:128], ssk
                        for b in range(2):
                            mm(po2[b * 64:(b + 1) * 64, :], qmp[b * 64:(b + 1) * 64, sl * 64:(sl + 1) * 64],
                               SP[b * 64:(b + 1) * 64, :], True, False, tp=(b * 64, b * 64),
                               r=[("qmp", sl), ("sp", h_)], w=[po2k])
                            mm(ps2_[b * 64:(b + 1) * 64, :], mst[b * 64:(b + 1) * 64, sl * 64:(sl + 1) * 64],
                               SP[b * 64:(b + 1) * 64, :], False, True, tp=(b * 64, b * 64),
                               r=[("mst", sl), ("sp", h_)], w=[ps2k])
                        vo(lambda h, ps2_=ps2_, sl=sl, h_=h_: h.tensor_tensor(
                            spair[:, h_ * 64:(h_ + 1) * 64], ps2_[:, 0:64],
                            css[:, sl * 64:(sl + 1) * 64], ALU.add),
                           r=[ps2k, ("css", sl)], w=[("sp", h_)])
                        OP = o_pair[:, h_ * 64:(h_ + 1) * 64]
                        vo(lambda h, po2=po2, OP=OP, sl=sl, h_=h_: h.scalar_tensor_tensor(
                            OP, po2[:, 0:64], dk(NQ8, h_, 1), p1s[:, sl * 64:(sl + 1) * 64],
                            ALU.mult, ALU.add),
                           r=[po2k, DKK, ("p1s", sl)],
                           w=[("op", h_)] + ([("wnat",)] if g == 0 and c == 0 and h_ == 0 else []))
                        so(lambda h, OP=OP: h.activation(
                            q_sg[:, 0:64], OP, AF.Square),
                           r=[("op", h_), ("qsg",)], w=[("qsg",)])
                        vo(lambda h, h_=h_: h.tensor_reduce(
                            dk(RR, h_, 1), q_sg[:, 0:64], AX.X, ALU.add),
                           r=[("qsg",)], w=[("rr", h_)])
            LBL[0] = 'oscale'
            vo(lambda h: h.tensor_scalar(dk(SC1), dk(RR), 1.0 / 64.0, None, ALU.mult),
               r=[("rr", hh2) for hh2 in range(16)] + [DKK], w=[DKK])
            so(lambda h: h.activation(dk(SC1), dk(SC1), AF.Sqrt, bias=epson[:]),
               r=[DKK, KC], w=[DKK])
            vo(lambda h: h.reciprocal(dk(OSC), dk(SC1)), r=[DKK], w=[DKK])
            for h_ in range(16):
                vo(lambda h, h_=h_: h.tensor_scalar(
                    o_pair[:, h_ * 64:(h_ + 1) * 64], o_pair[:, h_ * 64:(h_ + 1) * 64],
                    dk(OSC, h_, 1), None, ALU.mult), r=[DKK, ("op", h_)], w=[("op", h_)])
            for hp in range(8):
                pt2, ptk2 = rot_tr.get(128)
                tr(pt2, o_pair[:, hp * 128:(hp + 1) * 128], id128[:],
                   r=[("op", hp * 2), ("op", hp * 2 + 1), KC], w=[ptk2])
                so(lambda h, pt2=pt2, hp=hp, c=c: h.copy(
                    pair(oT, hp * GC, c), pt2[:].rearrange("p (b r) -> p b r", b=2)),
                   r=[ptk2], w=[("oT", hp, c)])

        if g == 0: t.mark['g0_delta'] = len(t.sched)

        LBL[0] = 'wo'
        for b in range(2):
            for half in range(2):
                pg, pgk = rot_big.get(512)
                for ic in range(8):
                    mm(pg, oT[:, ic * GC + b * 128: ic * GC + b * 128 + 128],
                       woT[:, ic * 1024 + half * 512: ic * 1024 + (half + 1) * 512],
                       ic == 0, ic == 7,
                       r=[("oT", ic, 0), ("oT", ic, 1)] + [("Wo", ic, o2) for o2 in range(8)],
                       w=[pgk])
                vo(lambda h, pg=pg, b=b, half=half, XT=XT: h.tensor_tensor(
                    x2[:, b * H + half * 512: b * H + (half + 1) * 512], pg,
                    XT[:, b * H + half * 512: b * H + (half + 1) * 512], ALU.add),
                   r=[pgk, XTK[b]], w=[("x2", b)])

        LBL[0] = 'ln2q'
        for b in range(2):
            emit_ln(("x2", b), x2[:, b * H:(b + 1) * H], b,
                    fstage[:, 0:H], ("fst",))
            quant(fstage[:, 0:H], xn_tm[:, b * H:(b + 1) * H], H,
                  [("fst",)], ("xn", b))
            bank, bkk = pbB[b], ("psb", 6 + b)
            for ic in range(8):
                tr(bank[:, ic * 128:(ic + 1) * 128],
                   xn_tm[:, b * H + ic * 128: b * H + (ic + 1) * 128], id128[:],
                   r=[("xn", b), KC], w=[bkk])
            so(lambda h, bank=bank, b=b: h.copy(
                xqT[:, :].rearrange("p (i g2) -> p i g2", g2=GC)[:, :, b * 128:(b + 1) * 128],
                bank[:, 0:1024].rearrange("p (i t) -> p i t", t=128)),
               r=[bkk], w=[("xqT", ic2, b) for ic2 in range(8)])

        if g == 0: t.mark['g0_ln2'] = len(t.sched)
        LBL[0] = 'ffngu'
        for oc4 in range(8):
            pr = oc4 % 2
            dma("SY", wgs[pr][:, :].rearrange("p (i c) -> p i c", c=352),
                wgT_h.rearrange("p (i c) -> p i c", c=IF)[:, :, oc4 * 352:(oc4 + 1) * 352],
                r=[("wgT_s", t2) for t2 in range(22)] if g == 0 and oc4 == 0 else [],
                w=[("wgs", pr)])
            dma("SY", wus[pr][:, :].rearrange("p (i c) -> p i c", c=352),
                wuT_h.rearrange("p (i c) -> p i c", c=IF)[:, :, oc4 * 352:(oc4 + 1) * 352],
                r=[("wuT_s", t2) for t2 in range(22)] if g == 0 and oc4 == 0 else [],
                w=[("wus", pr)])
            for b in range(2):
                pgt, pgtk = rot_big.get(352)
                put, putk = rot_big.get(352)
                for ic in range(8):
                    mm(pgt, xqT[:, ic * GC + b * 128: ic * GC + b * 128 + 128],
                       wgs[pr][:, ic * 352:(ic + 1) * 352], ic == 0, ic == 7,
                       r=[("xqT", ic, b), ("wgs", pr)], w=[pgtk])
                for ic in range(8):
                    mm(put, xqT[:, ic * GC + b * 128: ic * GC + b * 128 + 128],
                       wus[pr][:, ic * 352:(ic + 1) * 352], ic == 0, ic == 7,
                       r=[("xqT", ic, b), ("wus", pr)], w=[putk])
                so(lambda h, pgt=pgt: h.activation(fstage[:, 0:352], pgt, AF.Silu),
                   r=[pgtk, ("fst",)], w=[("fst",)])
                vo(lambda h, put=put: h.tensor_tensor(
                    fstage[:, 352:704], fstage[:, 0:352], put, ALU.mult),
                   r=[putk, ("fst",)], w=[("fst",)])
                hwk = [("hq", b, oc4)]
                if g == 0 and b == 0 and oc4 == 0:
                    hwk += [("qtile",), ("qtile", 0), ("qtile", 1), ("qtile", 2)]
                quant(fstage[:, 352:704],
                      h_tm[:, b * IF + oc4 * 352: b * IF + (oc4 + 1) * 352], 352,
                      [("fst",)], hwk)
        LBL[0] = 'hqtr'
        for b in range(2):
            for grp in range(3):
                k0 = grp * 8
                kn = min(8, 22 - k0)
                bank, bkk = pbB[grp % 2], ("psb", 6 + grp % 2)
                hkeys = set()
                for j in range(kn):
                    kc = k0 + j
                    p0, p1 = (kc * 128) // 352, (kc * 128 + 127) // 352
                    hkeys.add(p0); hkeys.add(p1)
                    tr(bank[:, j * 128:(j + 1) * 128],
                       h_tm[:, b * IF + kc * 128: b * IF + (kc + 1) * 128], id128[:],
                       r=[("hq", b, p0), ("hq", b, p1), KC], w=[bkk])
                so(lambda h, bank=bank, b=b, k0=k0, kn=kn: h.copy(
                    hqT[:, b * 22 * 128 + k0 * 128: b * 22 * 128 + (k0 + kn) * 128],
                    bank[:, 0:kn * 128]),
                   r=[bkk], w=[("hqT", b, kc2) for kc2 in range(k0, k0 + kn)] + (
                       [("stg", k2) for k2 in range(22)]
                       if (g == 0 and b == 0 and grp == 0) else []))
        LBL[0] = 'down'
        pd = [pbF[i][:, 0:512] for i in (0, 1, 4, 5)]
        pdk = [BK[i] for i in (0, 1, 4, 5)]
        for kc in range(22):
            pr = kc % 2
            dma("SY", wds[pr][:, :], wdT_h[:, kc * 1024:(kc + 1) * 1024],
                r=[("wdT_s", o2) for o2 in range(8)] if g == 0 and kc == 0 else [],
                w=[("wds", pr)])
            for b in range(2):
                for half in range(2):
                    mm(pd[b * 2 + half],
                       hqT[:, b * 22 * 128 + kc * 128: b * 22 * 128 + (kc + 1) * 128],
                       wds[pr][:, half * 512:(half + 1) * 512], kc == 0, kc == 21,
                       r=[("hqT", b, kc), ("wds", pr)], w=[pdk[b * 2 + half]])
        LBL[0] = 'outst'
        for b in range(2):
            for half in range(2):
                vo(lambda h, b=b, half=half: h.tensor_tensor(
                    out_t[:, 512:1024], pd[b * 2 + half],
                    x2[:, b * H + half * 512: b * H + (half + 1) * 512], ALU.add),
                   r=[pdk[b * 2 + half], ("x2", b), ("fst",)], w=[("fst",)])
                dma("GP", out_h[b, g * 128:(g + 1) * 128, half * 512:(half + 1) * 512],
                    out_t[:, 512:1024], r=[("fst",)], w=[("outd", g, b, half)])
        if DBG and g == int(os.environ.get('BASS_DBG_G', 0)):
            dma("GP", dbg_h[4, :, 0:2048], x2[:, 0:2048], r=[("x2", 0), ("x2", 1)], w=[("dbgd", 4)])
            dma("GP", dbg_h[1, :, 0:2816], h_tm[:, 0:IF],
                r=[("hq", 0, o4) for o4 in range(8)], w=[("dbgd", 1)])
            dma("GP", dbg_h[2, :, 0:1024], xn_tm[:, 0:H], r=[("xn", 0)], w=[("dbgd", 2)])

    trunc = os.environ.get("BASS_TRUNC")
    if trunc:
        n = t.mark[trunc] if trunc in t.mark else int(trunc)
        t.sched = t.sched[:n]
        t.cnt = dict.fromkeys(ENGS, 0)
        for eng_, fn_, w_, c_, p_ in t.sched:
            t.cnt[eng_] = max(t.cnt[eng_], c_)
        t.lw = {k: v for k, v in t.lw.items() if v[1] <= t.cnt[v[0]]}
        t.rd = {k: {e2: c2 for e2, c2 in d2.items() if c2 <= t.cnt[e2]}
                for k, d2 in t.rd.items()}
        print("TRUNC at", trunc, n, "marks:", t.mark)
        vo(lambda h: h.memset(fstage[:], 0.0), w=[("fst",)])
        for b_ in range(2):
            for r_ in range(16):
                dma("GP", out_h[b_, r_ * 128:(r_ + 1) * 128, :], fstage[:],
                    r=[("fst",)], w=[("outd", b_, r_)])
    t.op("VE", lambda h: h.nop(), xwaits={"GP": t.cnt["GP"], "SY": t.cnt["SY"]})

    per, remap, waited_dma = t.emit_plan()
    lp = nc.allow_low_precision(reason="bf16 elementwise sanctioned by problem tolerance")
    lp.__enter__()
    with (nc.semaphore("sTE") as sTE, nc.semaphore("sVE") as sVE,
          nc.semaphore("sSE") as sSE, nc.semaphore("sGP") as sGP,
          nc.semaphore("sSY") as sSY, nc.Block() as block):
        sems = {"TE": sTE, "VE": sVE, "SE": sSE, "GP": sGP, "SY": sSY}

        def emit_for(e, h):
            import kernel as _k
            dbg = getattr(_k, "_EMIT_LOG", None)
            dma_q = e in ("GP", "SY")
            k = 0
            for fn, waits, inc, par in per[e]:
                for e2, c2 in sorted(waits.items()):
                    h.wait_ge(sems[e2], remap(e2, c2))
                PAR_ACT[0] = par
                inst = fn(h)
                if dbg is not None:
                    try:
                        dbg.append((getattr(inst.ins, "name", None), e, waits))
                    except Exception:
                        pass
                if inc:
                    inst.then_inc(sems[e], 16 if dma_q else 1)
                if dma_q:
                    k += 1
                    if waited_dma[e] and k in waited_dma[e]:
                        h.wait_ge(sems[e], 16 * k)

        @block.tensor
        def _(eng):
            emit_for("TE", eng)

        @block.vector
        def _(eng):
            emit_for("VE", eng)

        @block.scalar
        def _(eng):
            emit_for("SE", eng)

        @block.gpsimd
        def _(eng):
            emit_for("GP", eng)

        @block.sync
        def _(eng):
            emit_for("SY", eng)

    lp.__exit__(None, None, None)
    for ccc in ctxs[::-1]:
        ccc.__exit__(None, None, None)
    build_program.last_t = t
    return nc


_CACHED = {}


def _consts():
    pidx = np.arange(128)[:, None] % 64
    j = np.arange(64)[None, :]
    return {
        "id128": np.eye(128, dtype=ml_dtypes.bfloat16),
        "su_stack": (j > pidx).astype(ml_dtypes.bfloat16),
        "sl_stack": (j < pidx).astype(ml_dtypes.bfloat16),
        "ui_stack": (j >= pidx).astype(ml_dtypes.bfloat16),
        "id_stack": (j == pidx).astype(ml_dtypes.bfloat16),
        "bdones": np.concatenate(
            [np.repeat([[1.0, 0.0]], 64, 0), np.repeat([[0.0, 1.0]], 64, 0)]
        ).astype(ml_dtypes.bfloat16),
    }


def kernel(L0_attn_x, L0_attn_Wq, L0_attn_Wk, L0_attn_Wv, L0_attn_Wb, L0_attn_Wo,
           L0_attn_conv_q, L0_attn_conv_k, L0_attn_conv_v, L0_attn_o_norm_w,
           L0_ffn_w_gate, L0_ffn_w_up, L0_ffn_w_down):
    if "nc" not in _CACHED:
        _CACHED["nc"] = build_program()
    nc = _CACHED["nc"]
    cst = _consts()
    shared = dict(
        Wq=np.ascontiguousarray(L0_attn_Wq), Wk=np.ascontiguousarray(L0_attn_Wk),
        Wv=np.ascontiguousarray(L0_attn_Wv), Wb=np.ascontiguousarray(L0_attn_Wb),
        Wo=np.ascontiguousarray(L0_attn_Wo),
        conv_q=np.ascontiguousarray(L0_attn_conv_q),
        conv_k=np.ascontiguousarray(L0_attn_conv_k),
        conv_v=np.ascontiguousarray(L0_attn_conv_v),
        o_norm_w=np.ascontiguousarray(L0_attn_o_norm_w),
        w_gate=np.ascontiguousarray(L0_ffn_w_gate),
        w_up=np.ascontiguousarray(L0_ffn_w_up),
        w_down=np.ascontiguousarray(L0_ffn_w_down),
        **cst,
    )
    in_maps = []
    for i in range(8):
        m = dict(shared)
        m["x"] = np.ascontiguousarray(L0_attn_x[2 * i:2 * i + 2])
        in_maps.append(m)
    res = run_bass_kernel_spmd(nc, in_maps, core_ids=list(range(8)))
    return np.concatenate([r["out"] for r in res.results], axis=0)
